# version 02
# baseline (speedup 1.0000x reference)
from contextlib import ExitStack
import numpy as np
import ml_dtypes
import concourse.bass as bass
import concourse.mybir as mybir
from concourse.bass_utils import run_bass_kernel_spmd

F32 = mybir.dt.float32
BF16 = mybir.dt.bfloat16
AF = mybir.ActivationFunctionType
OP = mybir.AluOpType
AX = mybir.AxisListType

T = 8192
D = 1024
NC_ = 8
DC = D // 128
NT = T // 128
LN_EPS = 1e-5

_bf16 = ml_dtypes.bfloat16


def _sb(nc, name, shape, dtype):
    return nc.sbuf_tensor(name, shape, dtype, allow_name_mangling=True)


def _ps(nc, name, shape, dtype=F32):
    return nc.psum_tensor(name, shape, dtype, allow_name_mangling=True)


class Sems:
    def __init__(self, nc, names):
        self.nc = nc
        self._ctx = []
        self.h = {}
        self.c = {}
        for n in names:
            cm = nc.semaphore(n)
            self.h[n] = cm.__enter__()
            self._ctx.append(cm)
            self.c[n] = 0

    def inc(self, inst, name, amt=1):
        inst.then_inc(self.h[name], amt)
        self.c[name] += amt
        return self.c[name]

    def get(self, key):
        return self.c.get(key, 0)

    def edge(self, eng, inst, name="vs"):
        inst.then_inc(self.h[name], 1)
        self.c[name] += 1
        eng.wait_ge(self.h[name], self.c[name])

    def close(self):
        for cm in reversed(self._ctx):
            cm.__exit__(None, None, None)


def host_consts():
    eye = np.eye(128)
    oneh = np.zeros((8, 8 * 128), np.float32)
    for e in range(8):
        oneh[e, e * 128:(e + 1) * 128] = 1.0
    return {
        "cst_eye_bf": eye.astype(_bf16),
        "cst_eye_f32": eye.astype(np.float32),
        "cst_oneh": oneh,
    }


def decl_consts(nc):
    return {
        "cst_eye_bf": nc.dram_tensor("cst_eye_bf", [128, 128], BF16, kind="ExternalInput").ap(),
        "cst_eye_f32": nc.dram_tensor("cst_eye_f32", [128, 128], F32, kind="ExternalInput").ap(),
        "cst_oneh": nc.dram_tensor("cst_oneh", [8, 8 * 128], F32, kind="ExternalInput").ap(),
    }


def phase_barrier(block, s):
    target = s.get("bar") + 1
    @block.vector
    def _(v):
        for nm in ["qx", "qw", "qs", "tn", "ve", "ac", "po"]:
            if s.c[nm] > 0:
                v.wait_ge(s.h[nm], s.c[nm])
        v.sem_inc(s.h["bar"], 1)
    s.c["bar"] = target
    for eng in ["sync", "gpsimd", "tensor", "scalar", "vector"]:
        @getattr(block, eng)
        def _(x):
            x.wait_ge(s.h["bar"], target)


SEM_NAMES = ["qx", "qw", "qs", "tn", "ve", "ac", "po", "bar", "vs", "as_"]


def emit_ln_tile(block, s, src_ap, dst_f32, sq, st16, eps=LN_EPS, ld_sem=None, ld_cnt=0,
                 rms_w=None, pre_wait=None):
    @block.scalar
    def _(a):
        if ld_sem is not None:
            a.wait_ge(s.h[ld_sem], ld_cnt)
        if pre_wait is not None:
            pre_wait(a)
        s.inc(a.activation(sq[:], src_ap, AF.Square, accum_out=st16[:, 0:1]), "ac")
    sq_c = s.c["ac"]
    @block.vector
    def _(v):
        if ld_sem is not None:
            v.wait_ge(s.h[ld_sem], ld_cnt)
        if pre_wait is not None:
            pre_wait(v)
        if rms_w is None:
            s.edge(v, v.reduce_sum(st16[:, 4:5], src_ap, axis=AX.X))
            s.edge(v, v.tensor_scalar_mul(st16[:, 5:6], st16[:, 4:5], -1.0 / D))
            s.edge(v, v.tensor_tensor(st16[:, 6:7], st16[:, 5:6], st16[:, 5:6], OP.mult))
            v.wait_ge(s.h["ac"], sq_c)
            s.edge(v, v.tensor_scalar(st16[:, 7:8], st16[:, 0:1], 1.0 / D, eps, OP.mult, OP.add))
            s.edge(v, v.tensor_tensor(st16[:, 8:9], st16[:, 7:8], st16[:, 6:7], OP.subtract))
        else:
            v.wait_ge(s.h["ac"], sq_c)
            s.edge(v, v.tensor_scalar(st16[:, 8:9], st16[:, 0:1], 1.0 / D, eps, OP.mult, OP.add))
        s.inc(v.reciprocal(st16[:, 9:10], st16[:, 8:9]), "ve")
    rec_c = s.c["ve"]
    @block.scalar
    def _(a):
        a.wait_ge(s.h["ve"], rec_c)
        s.inc(a.activation(st16[:, 1:2], st16[:, 9:10], AF.Sqrt), "ac")
    rstd_c = s.c["ac"]
    @block.vector
    def _(v):
        v.wait_ge(s.h["ac"], rstd_c)
        if rms_w is None:
            s.inc(v.tensor_scalar(dst_f32, src_ap, st16[:, 5:6], st16[:, 1:2], OP.add, OP.mult), "ve")
        else:
            s.inc(v.tensor_scalar_mul(dst_f32, src_ap, st16[:, 1:2]), "ve")
    return s.c["ve"]


def emit_moe(nc, block, s, cst, x_in, x_out, W, kind, scratch):
    xn_hbm = scratch["xn"]
    acc_hbm = scratch["acc"]
    wem_hbm = scratch["wem"]

    NE = 8
    F = 4096
    NSB = T // 512
    NG = 16

    with ExitStack() as _es:
        _e = _es.enter_context
        xb = _e(_sb(nc, "A_xb", [128, 2, 1024], F32))
        sq = _e(_sb(nc, "A_sq", [128, 1024], F32))
        st16 = _e(_sb(nc, "A_st", [128, 32], F32))
        xnf = _e(_sb(nc, "A_xnf", [128, 1024], F32))
        xnb = _e(_sb(nc, "A_xnb", [128, 2, 1024], BF16))
        xlo = _e(_sb(nc, "A_xlo", [128, 2, 1024], BF16))
        xnfm = _e(_sb(nc, "A_xnfm", [128, 2, 8 * 512], BF16))
        xlofm = _e(_sb(nc, "A_xlofm", [128, 1024], BF16))
        eyebf = _e(_sb(nc, "A_eyebf", [128, 128], BF16))
        eyef = _e(_sb(nc, "A_eyef", [128, 128], F32))
        wrh = _e(_sb(nc, "A_wrh", [128, 8 * 8], BF16))
        wrl = _e(_sb(nc, "A_wrl", [128, 8 * 8], BF16))
        wrf = _e(_sb(nc, "A_wrf", [128, 8 * 8], F32))
        rb = _e(_sb(nc, "A_rb", [8, 1], F32))
        lg = _e(_sb(nc, "A_lg", [8, 2, 512], F32))
        wem = _e(_sb(nc, "A_wem", [8, 2, 512], F32))
        t2 = _e(_sb(nc, "A_t2", [128, 2, 6 * 32], F32))
        t1 = _e(_sb(nc, "A_t1", [128, 2, 6 * 4], F32))
        wtm = _e(_sb(nc, "A_wtm", [128, 2, 32], F32))
        psT = _e(_ps(nc, "A_psT", [128, 1024], BF16))
        psT2 = _e(_ps(nc, "A_psT2", [128, 1024], BF16))
        psL = _e(_ps(nc, "A_psL", [8, 2, 512], F32))
        psLt = _e(_ps(nc, "A_psLt", [128, 2, 32], F32))
        psW = _e(_ps(nc, "A_psW", [8, 2, 512], F32))
        @block.sync
        def _(q):
            s.inc(q.dma_start(out=eyebf[:], in_=cst["cst_eye_bf"][:, :]), "qx", 16)
            s.inc(q.dma_start(out=eyef[:], in_=cst["cst_eye_f32"][:, :]), "qx", 16)
            s.inc(q.dma_start(out=wrf[:].rearrange("p (c e) -> p c e", c=DC), in_=W["router_w"].rearrange("(c p) e -> p c e", p=128)), "qx", 16)
            s.inc(q.dma_start(out=rb[:, 0], in_=W["router_b"][:]), "qx", 16)
        prep_qx = s.c["qx"]
        @block.vector
        def _(v):
            v.wait_ge(s.h["qx"], prep_qx)
            v.tensor_copy(wrh[:], wrf[:])
            s.inc(v.tensor_tensor(wrl[:], wrf[:], wrh[:], OP.subtract), "ve")
        prep_ve = s.c["ve"]

        for g in range(NG):
            gslot = g % 2
            for ti in range(4):
                t = g * 4 + ti
                tslot = t % 2
                @block.sync
                def _(q):
                    q.wait_ge(s.h["ve"], s.get(f"xb_used_{tslot}"))
                    s.inc(q.dma_start(out=xb[:, tslot, :], in_=x_in[t * 128:(t + 1) * 128, :]), "qx", 16)
                ld_c = s.c["qx"]

                emit_ln_tile(block, s, xb[:, tslot, :], xnf[:], sq,
                             st16[:, tslot * 16:(tslot + 1) * 16], ld_sem="qx", ld_cnt=ld_c)
                s.c[f"xb_used_{tslot}"] = s.c["ve"]
                @block.vector
                def _(v):
                    v.wait_ge(s.h["tn"], s.get(f"xnb_used_{tslot}"))
                    s.edge(v, v.tensor_copy(xnb[:, tslot, :], xnf[:]))
                    s.inc(v.tensor_tensor(xlo[:, tslot, :], xnf[:], xnb[:, tslot, :], OP.subtract), "ve")
                xn_c = s.c["ve"]

                @block.tensor
                def _(tn):
                    tn.wait_ge(s.h["ve"], xn_c)
                    tn.wait_ge(s.h["ve"], s.get("psT_used"))
                    for c in range(DC):
                        tn.transpose(psT[:, c * 128:(c + 1) * 128],
                                     xnb[:, tslot, c * 128:(c + 1) * 128], eyebf[:])
                    for c in range(DC):
                        i = tn.transpose(psT2[:, c * 128:(c + 1) * 128],
                                         xlo[:, tslot, c * 128:(c + 1) * 128], eyebf[:])
                    s.inc(i, "tn")
                s.c[f"xnb_used_{tslot}"] = s.c["tn"]
                tr_c = s.c["tn"]

                @block.vector
                def _(v):
                    v.wait_ge(s.h["tn"], tr_c)
                    v.wait_ge(s.h["tn"], s.get("xlofm_used"))
                    if ti == 0:
                        v.wait_ge(s.h["qx"], s.get(f"xnfm_used_{gslot}"))
                    v.tensor_copy(
                        xnfm[:, gslot, :].rearrange("p (c l) -> p c l", c=DC)[:, :, ti * 128:(ti + 1) * 128],
                        psT[:].rearrange("p (c l) -> p c l", c=DC))
                    s.inc(v.tensor_copy(
                        xlofm[:].rearrange("p (c l) -> p c l", c=DC),
                        psT2[:].rearrange("p (c l) -> p c l", c=DC)), "ve")
                    s.c["psT_used"] = s.c["ve"]
                cp_c = s.c["ve"]

                @block.tensor
                def _(tn):
                    tn.wait_ge(s.h["ve"], cp_c)
                    tn.wait_ge(s.h["ve"], prep_ve)
                    if ti == 0:
                        tn.wait_ge(s.h["ac"], s.get(f"psL_used_{gslot}"))
                    dst = psL[:, gslot, ti * 128:(ti + 1) * 128]
                    xg = xnfm[:, gslot, :].rearrange("p (c l) -> p c l", c=DC)
                    xlg = xlofm[:].rearrange("p (c l) -> p c l", c=DC)
                    first = True
                    for c in range(DC):
                        xh = xg[:, c, ti * 128:(ti + 1) * 128]
                        xl = xlg[:, c, :]
                        tn.matmul(dst, wrh[:, c * 8:(c + 1) * 8], xh, start=first, stop=False); first = False
                        tn.matmul(dst, wrl[:, c * 8:(c + 1) * 8], xh, start=False, stop=False)
                        i = tn.matmul(dst, wrh[:, c * 8:(c + 1) * 8], xl, start=False, stop=(c == DC - 1))
                    s.inc(i, "tn")
                    s.c["xlofm_used"] = s.c["tn"]
            lgmm_c = s.c["tn"]

            @block.scalar
            def _(a):
                a.wait_ge(s.h["tn"], lgmm_c)
                s.inc(a.activation(lg[:, gslot, :], psL[:, gslot, :], AF.Identity, bias=rb[:, 0:1]), "ac")
                s.c[f"psL_used_{gslot}"] = s.c["ac"]
            lge_c = s.c["ac"]
            @block.tensor
            def _(tn):
                tn.wait_ge(s.h["ac"], lge_c)
                tn.wait_ge(s.h["ve"], s.get(f"psLt_used_{gslot}"))
                for ti in range(4):
                    i = tn.transpose(psLt[:, gslot, ti * 8:(ti + 1) * 8],
                                     lg[:, gslot, ti * 128:(ti + 1) * 128], eyef[0:8, 0:8])
                s.inc(i, "tn")
            lt_c = s.c["tn"]
            @block.vector
            def _(v):
                v.wait_ge(s.h["tn"], lt_c)
                lgt = psLt[:, gslot, :].rearrange("p (ti e) -> p ti e", e=8)
                B2 = lambda i: t2[:, gslot, i * 32:(i + 1) * 32].rearrange("p (ti e) -> p ti e", e=8)
                S1 = lambda i: t1[:, gslot, i * 4:(i + 1) * 4]
                s.edge(v, v.tensor_copy(B2(0), lgt))
                l0 = t2[:, gslot, 0 * 32:1 * 32].rearrange("p (ti e) -> p ti e", e=8)
                s.edge(v, v.reduce_max(S1(0)[:, :, None], l0, axis=AX.X))
                s.edge(v, v.tensor_tensor(B2(1), l0, S1(0)[:, :, None].broadcast_to([128, 4, 8]), OP.subtract))
                l1 = t2[:, gslot, 1 * 32:2 * 32].rearrange("p (ti e) -> p ti e", e=8)
                s.edge(v, v.tensor_single_scalar(B2(2), l1, 0.0, OP.is_equal))
                l2 = t2[:, gslot, 2 * 32:3 * 32].rearrange("p (ti e) -> p ti e", e=8)
                s.edge(v, v.scalar_tensor_tensor(B2(3), l2, -1e30, l1, OP.mult, OP.add))
                l3 = t2[:, gslot, 3 * 32:4 * 32].rearrange("p (ti e) -> p ti e", e=8)
                s.edge(v, v.reduce_max(S1(1)[:, :, None], l3, axis=AX.X))
                s.edge(v, v.tensor_tensor(B2(4), l3, S1(1)[:, :, None].broadcast_to([128, 4, 8]), OP.subtract))
                l4 = t2[:, gslot, 4 * 32:5 * 32].rearrange("p (ti e) -> p ti e", e=8)
                s.edge(v, v.tensor_single_scalar(B2(5), l4, 0.0, OP.is_equal))
                l5 = t2[:, gslot, 5 * 32:6 * 32].rearrange("p (ti e) -> p ti e", e=8)
                s.inc(v.tensor_tensor(B2(4), l2, l5, OP.add), "ve")
            msk_c = s.c["ve"]
            @block.scalar
            def _(a):
                a.wait_ge(s.h["ve"], msk_c)
                a.activation(t2[:, gslot, 3 * 32:4 * 32], t2[:, gslot, 1 * 32:2 * 32], AF.Exp)
                s.inc(a.activation(t1[:, gslot, 2 * 4:3 * 4], t1[:, gslot, 1 * 4:2 * 4], AF.Exp), "ac")
            ex_c = s.c["ac"]
            @block.vector
            def _(v):
                v.wait_ge(s.h["ac"], ex_c)
                s.edge(v, v.tensor_single_scalar(t1[:, gslot, 3 * 4:4 * 4], t1[:, gslot, 2 * 4:3 * 4], 1.0, OP.add))
                s.edge(v, v.reciprocal(t1[:, gslot, 4 * 4:5 * 4], t1[:, gslot, 3 * 4:4 * 4]))
                S1 = lambda i: t1[:, gslot, i * 4:(i + 1) * 4]
                exp_ = t2[:, gslot, 3 * 32:4 * 32].rearrange("p (ti e) -> p ti e", e=8)
                msk_ = t2[:, gslot, 4 * 32:5 * 32].rearrange("p (ti e) -> p ti e", e=8)
                wt_ = wtm[:, gslot, :].rearrange("p (ti e) -> p ti e", e=8)
                s.edge(v, v.tensor_tensor(wt_, exp_, msk_, OP.mult))
                s.inc(v.tensor_tensor(wt_, wt_, S1(4)[:, :, None].broadcast_to([128, 4, 8]), OP.mult), "ve")
                s.c[f"psLt_used_{gslot}"] = s.c["ve"]
            w_c = s.c["ve"]
            @block.tensor
            def _(tn):
                tn.wait_ge(s.h["ve"], w_c)
                tn.wait_ge(s.h["ve"], s.get(f"psW_used_{gslot}"))
                for ti in range(4):
                    i = tn.transpose(psW[:, gslot, ti * 128:(ti + 1) * 128],
                                     wtm[:, gslot, ti * 8:(ti + 1) * 8], eyef[:])
                s.inc(i, "tn")
            wt_c = s.c["tn"]
            @block.vector
            def _(v):
                v.wait_ge(s.h["tn"], wt_c)
                v.wait_ge(s.h["qx"], s.get(f"wem_st_{gslot}"))
                s.inc(v.tensor_copy(wem[:, gslot, :], psW[:, gslot, :]), "ve")
                s.c[f"psW_used_{gslot}"] = s.c["ve"]
            wev_c = s.c["ve"]
            @block.sync
            def _(q):
                q.wait_ge(s.h["ve"], wev_c)
                s.inc(q.dma_start(out=wem_hbm[:, g * 512:(g + 1) * 512], in_=wem[:, gslot, :]), "qx", 16)
                s.c[f"wem_st_{gslot}"] = s.c["qx"]
                s.inc(q.dma_start(
                    out=xn_hbm.rearrange("(c p) l -> p c l", p=128)[:, :, g * 512:(g + 1) * 512],
                    in_=xnfm[:, gslot, :].rearrange("p (c l) -> p c l", c=DC)), "qx", 16)
                s.c[f"xnfm_used_{gslot}"] = s.c["qx"]

    phase_barrier(block, s)

    specs = [("e", e) for e in range(NE)] + ([("sh", 0)] if kind == "l1" else [])
    with ExitStack() as _es:
        _e = _es.enter_context
        w1s = _e(_sb(nc, "E_w1", [128, 8 * F], BF16))
        w2s = _e(_sb(nc, "E_w2", [128, 32 * 1024], BF16))
        hs = _e(_sb(nc, "E_h", [128, 32 * 512], BF16))
        xns = _e(_sb(nc, "E_xn", [128, 2, 8 * 512], BF16))
        tmps = _e(_sb(nc, "E_tmp", [128, 2, 4 * 512], F32))
        wems = _e(_sb(nc, "E_wem", [8, 2, 512], BF16))
        wbs = _e(_sb(nc, "E_wb", [128, 2, 512], BF16))
        onehs = _e(_sb(nc, "E_oneh", [8, 8 * 128], BF16))
        b1s = _e(_sb(nc, "E_b1", [128, 32], F32))
        b2s = _e(_sb(nc, "E_b2", [128, 8], F32))
        g1s = _e(_sb(nc, "E_g1", [128, 32], F32))
        pgs = _e(_sb(nc, "E_pg", [128, 32], F32))
        sws = _e(_sb(nc, "E_sw", [1, 8], F32))
        swbs = _e(_sb(nc, "E_swb", [128, 1], F32))
        onep = _e(_sb(nc, "E_onep", [1, 128], F32))
        psA = _e(_ps(nc, "E_psA", [128, 4, 512], F32))
        psB = _e(_ps(nc, "E_psB", [128, 3, 512], F32))
        psw = _e(_ps(nc, "E_psw", [128, 512], F32))
        @block.gpsimd
        def _(gq):
            s.inc(gq.dma_start(out=onehs[:], in_=cst["cst_oneh"][:, :]), "qw", 16)
        @block.sync
        def _(q):
            if kind == "l1":
                s.inc(q.dma_start(out=sws[0:1, 0:1], in_=W["shared_weight"][None, :]), "qx", 16)
                s.inc(q.dma_start(out=onep[0:1, :], in_=cst["cst_oneh"][0:1, 0:128]), "qx", 16)
        prep2_qx = s.c["qx"]
        oneh_qw = s.c["qw"]
        if kind == "l1":
            @block.scalar
            def _(a):
                a.wait_ge(s.h["qx"], prep2_qx)
                s.inc(a.activation(sws[0:1, 1:2], sws[0:1, 0:1], AF.Sigmoid), "ac")
            sg_c = s.c["ac"]
            @block.tensor
            def _(tn):
                tn.wait_ge(s.h["ac"], sg_c)
                s.inc(tn.matmul(psw[:, 0:1], onep[0:1, :], sws[0:1, 1:2], start=True, stop=True), "tn")
            sb_c = s.c["tn"]
            @block.vector
            def _(v):
                v.wait_ge(s.h["tn"], sb_c)
                s.inc(v.tensor_copy(swbs[:, 0:1], psw[:, 0:1]), "ve")
        swb_ve = s.c["ve"]

        for ei, (ekind, e) in enumerate(specs):
            is_sh = (ekind == "sh")
            FE = 2048 if is_sh else F
            NM = FE // 128
            NK2 = FE // 128
            @block.gpsimd
            def _(gq):
                if ei > 0:
                    gq.wait_ge(s.h["tn"], s.get("w_used_tn"))
                    gq.wait_ge(s.h["ac"], s.c["ac"])
                w1_src = W["sw1"] if is_sh else W["w1"][e]
                w2_src = W["sw2"] if is_sh else W["w2"][e]
                for k in range(DC):
                    s.inc(gq.dma_start(out=w1s[:, k * F:k * F + FE],
                                       in_=w1_src.rearrange("(k p) f -> k p f", p=128)[k]), "qw", 16)
                for k2 in range(NK2):
                    s.inc(gq.dma_start(out=w2s[:, k2 * 1024:(k2 + 1) * 1024],
                                       in_=w2_src.rearrange("(k p) d -> k p d", p=128)[k2]), "qw", 16)
                b1_src = W["sb1"] if is_sh else W["b1"][e]
                b2_src = W["sb2"] if is_sh else W["b2"][e]
                s.inc(gq.dma_start(out=b1s[:, 0:NM], in_=b1_src.rearrange("(m p) -> p m", p=128), allow_slow_non_contiguous=True), "qw", 16)
                s.inc(gq.dma_start(out=b2s[:, :], in_=b2_src.rearrange("(m p) -> p m", p=128), allow_slow_non_contiguous=True), "qw", 16)
                if kind == "l1":
                    g_src = W["sgate"] if is_sh else W["gate"][e]
                    s.inc(gq.dma_start(out=g1s[:, 0:NM], in_=g_src.rearrange("(m p) -> p m", p=128), allow_slow_non_contiguous=True), "qw", 16)
            wl_c = s.c["qw"]
            pg_c = 0
            if kind == "l1":
                @block.vector
                def _(v):
                    v.wait_ge(s.h["qw"], wl_c)
                    s.inc(v.tensor_tensor(pgs[:, 0:NM], b1s[:, 0:NM], g1s[:, 0:NM], OP.mult), "ve")
                pg_c = s.c["ve"]

            for sb in range(NSB):
                sslot = sb % 2
                @block.sync
                def _(q):
                    q.wait_ge(s.h["tn"], s.get(f"xn_used_{sslot}"))
                    s.inc(q.dma_start(out=xns[:, sslot, :],
                                      in_=xn_hbm.rearrange("(c p) l -> p c l", p=128)[:, :, sb * 512:(sb + 1) * 512]), "qx", 16)
                xl_c = s.c["qx"]
                if not is_sh:
                    @block.gpsimd
                    def _(gq):
                        gq.wait_ge(s.h["tn"], s.get(f"wem_used_{sslot}"))
                        s.inc(gq.dma_start(out=wems[:, sslot, :], in_=wem_hbm[:, sb * 512:(sb + 1) * 512]), "qw", 16)
                wem_c = s.c["qw"]

                wbv_c = 0
                if not is_sh:
                    @block.tensor
                    def _(tn):
                        tn.wait_ge(s.h["qw"], wem_c)
                        tn.wait_ge(s.h["qw"], oneh_qw)
                        tn.wait_ge(s.h["ve"], s.get(f"psw_used"))
                        i = tn.matmul(psw[:], onehs[0:8, e * 128:(e + 1) * 128], wems[:, sslot, :],
                                      start=True, stop=True)
                        s.inc(i, "tn")
                        s.c[f"wem_used_{sslot}"] = s.c["tn"]
                    wb_c = s.c["tn"]
                    @block.vector
                    def _(v):
                        v.wait_ge(s.h["tn"], wb_c)
                        v.wait_ge(s.h["ve"], s.get(f"wb_used_{sslot}"))
                        s.inc(v.tensor_copy(wbs[:, sslot, :], psw[:]), "ve")
                        s.c["psw_used"] = s.c["ve"]
                    wbv_c = s.c["ve"]

                h_ac_base = s.c["ac"]
                @block.tensor
                def _(tn):
                    tn.wait_ge(s.h["qw"], wl_c)
                    tn.wait_ge(s.h["qx"], xl_c)
                    tn.wait_ge(s.h["ac"], s.get("h_used_ac"))
                    for m in range(NM):
                        if m >= 4:
                            tn.wait_ge(s.h["ac"], h_ac_base + m - 3)
                        for k in range(DC):
                            i = tn.matmul(psA[:, m % 4, :], w1s[:, k * F + m * 128: k * F + m * 128 + 128],
                                          xns[:, sslot, k * 512:(k + 1) * 512],
                                          start=(k == 0), stop=(k == DC - 1))
                        s.inc(i, "tn")
                    s.c[f"xn_used_{sslot}"] = s.c["tn"]
                w1_base = s.c["tn"] - NM
                @block.scalar
                def _(a):
                    for m in range(NM):
                        a.wait_ge(s.h["tn"], w1_base + m + 1)
                        if kind == "l1":
                            if pg_c:
                                a.wait_ge(s.h["ve"], pg_c)
                            i = a.activation(hs[:, m * 512:(m + 1) * 512], psA[:, m % 4, :], AF.Gelu,
                                             bias=pgs[:, m:m + 1], scale=g1s[:, m:m + 1])
                        else:
                            i = a.activation(hs[:, m * 512:(m + 1) * 512], psA[:, m % 4, :], AF.Relu,
                                             bias=b1s[:, m:m + 1])
                        s.inc(i, "ac")
                h_done = s.c["ac"]

                y_ve_base = s.c["ve"]
                @block.tensor
                def _(tn):
                    tn.wait_ge(s.h["ac"], h_done)
                    if not is_sh:
                        tn.wait_ge(s.h["ve"], wbv_c)
                    if kind == "l1" and is_sh:
                        tn.wait_ge(s.h["ve"], swb_ve)
                    for d in range(DC):
                        if d >= 3:
                            tn.wait_ge(s.h["ve"], y_ve_base + d - 2)
                        for k2 in range(NK2):
                            i = tn.matmul(psB[:, d % 3, :], w2s[:, k2 * 1024 + d * 128: k2 * 1024 + d * 128 + 128],
                                          hs[:, k2 * 512:(k2 + 1) * 512],
                                          start=(k2 == 0), stop=(k2 == NK2 - 1))
                        s.inc(i, "tn")
                w2_base = s.c["tn"] - DC
                s.c["w_used_tn"] = s.c["tn"]
                s.c["h_used_ac"] = 0
                for half in range(2):
                    hslot = half
                    @block.vector
                    def _(v):
                        for dd in range(4):
                            d = half * 4 + dd
                            v.wait_ge(s.h["tn"], w2_base + d + 1)
                            if dd == 0:
                                v.wait_ge(s.h["qw"], s.get(f"tmp_st_{hslot}"))
                            if is_sh:
                                i = v.tensor_scalar(tmps[:, hslot, dd * 512:(dd + 1) * 512], psB[:, d % 3, :],
                                                    b2s[:, d:d + 1], swbs[:, 0:1], OP.add, OP.mult)
                            else:
                                i = v.scalar_tensor_tensor(tmps[:, hslot, dd * 512:(dd + 1) * 512], psB[:, d % 3, :],
                                                           b2s[:, d:d + 1], wbs[:, sslot, :], OP.add, OP.mult)
                            s.inc(i, "ve")
                        if half == 1:
                            s.c[f"wb_used_{sslot}"] = s.c["ve"]
                    y_done = s.c["ve"]
                    @block.gpsimd
                    def _(gq):
                        gq.wait_ge(s.h["ve"], y_done)
                        s.inc(gq.dma_start(
                            out=acc_hbm.rearrange("(c p) l -> p c l", p=128)[:, half * 4:(half + 1) * 4,
                                                                            sb * 512:(sb + 1) * 512],
                            in_=tmps[:, hslot, :].rearrange("p (c l) -> p c l", c=4),
                            accum_op=(OP.bypass if ei == 0 else OP.add)), "qw", 16)
                        s.c[f"tmp_st_{hslot}"] = s.c["qw"]

    phase_barrier(block, s)

    emit_residual(nc, block, s, cst, acc_hbm, x_in, x_out, 0.5,
                  ln_aff=(W["ln_g"], W["ln_b"]) if kind == "l1" else None)


def emit_residual(nc, block, s, cst, acc_hbm, x_in, x_out, scale, ln_aff=None, add_hbm=None):
    with ExitStack() as _es:
        _e = _es.enter_context
        accs = _e(_sb(nc, "R_acc", [128, 2, 8 * 512], F32))
        rxs = _e(_sb(nc, "R_x", [128, 2, 4 * 1024], F32))
        rx2 = _e(_sb(nc, "R_x2", [128, 2, 4 * 1024], F32))
        eyef2 = _e(_sb(nc, "R_eyef", [128, 128], F32))
        at_ = _e(_sb(nc, "R_at", [128, 1024], F32))
        rsq = _e(_sb(nc, "R_sq", [128, 1024], F32))
        rst = _e(_sb(nc, "R_st", [128, 32], F32))
        lng = _e(_sb(nc, "R_g", [128, 1024], F32))
        lnb = _e(_sb(nc, "R_b", [128, 1024], F32))
        grow = _e(_sb(nc, "R_grow", [1, 1024], F32))
        brow = _e(_sb(nc, "R_brow", [1, 1024], F32))
        ronep = _e(_sb(nc, "R_onep", [1, 128], F32))
        xos = _e(_sb(nc, "R_xo", [128, 2, 4 * 1024], F32))
        psR = _e(_ps(nc, "R_ps", [128, 2, 1024], F32))
        @block.sync
        def _(q):
            s.inc(q.dma_start(out=eyef2[:], in_=cst["cst_eye_f32"][:, :]), "qx", 16)
            if ln_aff is not None:
                s.inc(q.dma_start(out=grow[0:1, :], in_=ln_aff[0][None, :]), "qx", 16)
                s.inc(q.dma_start(out=brow[0:1, :], in_=ln_aff[1][None, :]), "qx", 16)
                s.inc(q.dma_start(out=ronep[0:1, :], in_=cst["cst_oneh"][0:1, 0:128]), "qx", 16)
        pr_c = s.c["qx"]
        if ln_aff is not None:
            @block.tensor
            def _(tn):
                tn.wait_ge(s.h["qx"], pr_c)
                for hh in range(2):
                    tn.matmul(psR[:, 0, hh * 512:(hh + 1) * 512], ronep[0:1, :],
                              grow[0:1, hh * 512:(hh + 1) * 512], start=True, stop=True)
                    i = tn.matmul(psR[:, 1, hh * 512:(hh + 1) * 512], ronep[0:1, :],
                                  brow[0:1, hh * 512:(hh + 1) * 512], start=True, stop=True)
                s.inc(i, "tn")
            gb_c = s.c["tn"]
            @block.vector
            def _(v):
                v.wait_ge(s.h["tn"], gb_c)
                s.edge(v, v.tensor_copy(lng[:], psR[:, 0, :]))
                s.inc(v.tensor_copy(lnb[:], psR[:, 1, :]), "ve")
                s.c["psR_used_0"] = s.c["ve"]
                s.c["psR_used_1"] = s.c["ve"]
            gbv_c = s.c["ve"]
        for g in range(16):
            gslot = g % 2
            @block.sync
            def _(q):
                q.wait_ge(s.h["qx"], pr_c)
                q.wait_ge(s.h["qs"], s.get(f"xo_st_{gslot}"))
                s.inc(q.dma_start(out=accs[:, gslot, :],
                                  in_=acc_hbm.rearrange("(c p) l -> p c l", p=128)[:, :, g * 512:(g + 1) * 512]), "qx", 16)
                s.inc(q.dma_start(out=rxs[:, gslot, :].rearrange("p (ti f) -> p ti f", ti=4),
                                  in_=x_in[g * 512:(g + 1) * 512, :].rearrange("(ti p) f -> p ti f", p=128)), "qx", 16)
                if add_hbm is not None:
                    s.inc(q.dma_start(out=rx2[:, gslot, :].rearrange("p (ti f) -> p ti f", ti=4),
                                      in_=add_hbm[g * 512:(g + 1) * 512, :].rearrange("(ti p) f -> p ti f", p=128)), "qx", 16)
            rl_c = s.c["qx"]
            for ti in range(4):
                t4 = g * 4 + ti
                pslot = t4 % 2
                @block.tensor
                def _(tn):
                    tn.wait_ge(s.h["qx"], rl_c)
                    tn.wait_ge(s.h["ve"], s.get(f"psR_used_{pslot}"))
                    for c in range(DC):
                        i = tn.transpose(psR[:, pslot, c * 128:(c + 1) * 128],
                                         accs[:, gslot, :].rearrange("p (c l) -> p c l", c=DC)[:, c, ti * 128:(ti + 1) * 128],
                                         eyef2[:])
                    s.inc(i, "tn")
                rt_c = s.c["tn"]
                if ln_aff is not None:
                    emit_ln_tile(block, s, psR[:, pslot, :], at_[:], rsq,
                                 rst[:, pslot * 16:(pslot + 1) * 16],
                                 ld_sem="tn", ld_cnt=rt_c)
                    ln_c = s.c["ve"]
                    @block.vector
                    def _(v):
                        v.wait_ge(s.h["ve"], gbv_c)
                        s.edge(v, v.tensor_tensor(at_[:], at_[:], lng[:], OP.mult))
                        s.edge(v, v.tensor_tensor(at_[:], at_[:], lnb[:], OP.add))
                        s.inc(v.scalar_tensor_tensor(xos[:, gslot, ti * 1024:(ti + 1) * 1024],
                                                     at_[:], scale, rxs[:, gslot, ti * 1024:(ti + 1) * 1024],
                                                     OP.mult, OP.add), "ve")
                        s.c[f"psR_used_{pslot}"] = s.c["ve"]
                else:
                    @block.vector
                    def _(v):
                        v.wait_ge(s.h["tn"], rt_c)
                        s.inc(v.scalar_tensor_tensor(xos[:, gslot, ti * 1024:(ti + 1) * 1024],
                                                     psR[:, pslot, :], scale,
                                                     rxs[:, gslot, ti * 1024:(ti + 1) * 1024],
                                                     OP.mult, OP.add), "ve")
                        s.c[f"psR_used_{pslot}"] = s.c["ve"]
                if add_hbm is not None:
                    @block.vector
                    def _(v):
                        s.inc(v.tensor_tensor(xos[:, gslot, ti * 1024:(ti + 1) * 1024],
                                              xos[:, gslot, ti * 1024:(ti + 1) * 1024],
                                              rx2[:, gslot, ti * 1024:(ti + 1) * 1024], OP.add), "ve")
            st_c = s.c["ve"]
            @block.scalar
            def _(a):
                a.wait_ge(s.h["ve"], st_c)
                s.inc(a.dma_start(out=x_out[g * 512:(g + 1) * 512, :].rearrange("(ti p) f -> p ti f", p=128),
                                  in_=xos[:, gslot, :].rearrange("p (ti f) -> p ti f", ti=4)), "qs", 16)
                s.c[f"xo_st_{gslot}"] = s.c["qs"]
    phase_barrier(block, s)


def moe_weight_decls(nc, kind, pfx=""):
    if kind == "l0":
        return {
            "router_w": nc.dram_tensor(pfx + "W_router", [D, 8], F32, kind="ExternalInput").ap(),
            "router_b": nc.dram_tensor(pfx + "b_router", [8], F32, kind="ExternalInput").ap(),
            "w1": nc.dram_tensor(pfx + "W1", [8, D, 4096], F32, kind="ExternalInput").ap(),
            "b1": nc.dram_tensor(pfx + "b1", [8, 4096], F32, kind="ExternalInput").ap(),
            "w2": nc.dram_tensor(pfx + "W2", [8, 4096, D], F32, kind="ExternalInput").ap(),
            "b2": nc.dram_tensor(pfx + "b2", [8, D], F32, kind="ExternalInput").ap(),
        }
    return {
        "router_w": nc.dram_tensor(pfx + "router_w", [D, 8], F32, kind="ExternalInput").ap(),
        "router_b": nc.dram_tensor(pfx + "router_b", [8], F32, kind="ExternalInput").ap(),
        "w1": nc.dram_tensor(pfx + "w1", [8, D, 4096], F32, kind="ExternalInput").ap(),
        "b1": nc.dram_tensor(pfx + "b1", [8, 4096], F32, kind="ExternalInput").ap(),
        "gate": nc.dram_tensor(pfx + "gate", [8, 4096], F32, kind="ExternalInput").ap(),
        "w2": nc.dram_tensor(pfx + "w2", [8, 4096, D], F32, kind="ExternalInput").ap(),
        "b2": nc.dram_tensor(pfx + "b2", [8, D], F32, kind="ExternalInput").ap(),
        "sw1": nc.dram_tensor(pfx + "sw1", [D, 2048], F32, kind="ExternalInput").ap(),
        "sb1": nc.dram_tensor(pfx + "sb1", [2048], F32, kind="ExternalInput").ap(),
        "sgate": nc.dram_tensor(pfx + "sgate", [2048], F32, kind="ExternalInput").ap(),
        "sw2": nc.dram_tensor(pfx + "sw2", [2048, D], F32, kind="ExternalInput").ap(),
        "sb2": nc.dram_tensor(pfx + "sb2", [D], F32, kind="ExternalInput").ap(),
        "shared_weight": nc.dram_tensor(pfx + "shared_weight", [1], F32, kind="ExternalInput").ap(),
        "ln_g": nc.dram_tensor(pfx + "ln_g", [D], F32, kind="ExternalInput").ap(),
        "ln_b": nc.dram_tensor(pfx + "ln_b", [D], F32, kind="ExternalInput").ap(),
    }


def moe_weight_inmap(inputs, kind, lyr):
    if kind == "l0":
        return {
            "W_router": inputs[f"L{lyr}_ffn_W_router"], "b_router": inputs[f"L{lyr}_ffn_b_router"],
            "W1": inputs[f"L{lyr}_ffn_W1"], "b1": inputs[f"L{lyr}_ffn_b1"],
            "W2": inputs[f"L{lyr}_ffn_W2"], "b2": inputs[f"L{lyr}_ffn_b2"],
        }
    return {
        "router_w": inputs[f"L{lyr}_ffn_router_w"], "router_b": inputs[f"L{lyr}_ffn_router_b"],
        "w1": inputs[f"L{lyr}_ffn_w1"], "b1": inputs[f"L{lyr}_ffn_b1"],
        "gate": inputs[f"L{lyr}_ffn_gate"], "w2": inputs[f"L{lyr}_ffn_w2"], "b2": inputs[f"L{lyr}_ffn_b2"],
        "sw1": inputs[f"L{lyr}_ffn_sw1"], "sb1": inputs[f"L{lyr}_ffn_sb1"],
        "sgate": inputs[f"L{lyr}_ffn_sgate"], "sw2": inputs[f"L{lyr}_ffn_sw2"],
        "sb2": inputs[f"L{lyr}_ffn_sb2"], "shared_weight": np.asarray(inputs[f"L{lyr}_ffn_shared_weight"]).reshape(1),
        "ln_g": inputs[f"L{lyr}_ffn_ln_g"], "ln_b": inputs[f"L{lyr}_ffn_ln_b"],
    }


def build_moe_launch(kind, debug=False):
    nc = bass.Bass("TRN2", target_bir_lowering=False)
    cst = decl_consts(nc)
    x_in = nc.dram_tensor("x_in", [T, D], F32, kind="ExternalInput").ap()
    x_out = nc.dram_tensor("x_out", [T, D], F32, kind="ExternalOutput").ap()
    W = moe_weight_decls(nc, kind)
    sk = "ExternalOutput" if debug else "Internal"
    scratch = {
        "xn": nc.dram_tensor("scr_xn", [D, T], BF16, kind=sk).ap(),
        "acc": nc.dram_tensor("scr_acc", [D, T], F32, kind=sk).ap(),
        "wem": nc.dram_tensor("scr_wem", [8, T], F32, kind=sk).ap(),
    }
    with nc.Block() as block:
        s = Sems(nc, SEM_NAMES)
        emit_moe(nc, block, s, cst, x_in, x_out, W, kind, scratch)
        @block.scalar
        def _(a):
            a.wait_ge(s.h["qs"], s.c["qs"])
        s.close()
    return nc



def ssm_consts():
    n1 = np.arange(128)
    Wm = np.exp(-2j * np.pi * np.outer(n1, n1) / 128.0)
    TW = np.exp(-2j * np.pi * np.outer(n1, n1) / 16384.0)
    return {
        "cst_wc": Wm.real.astype(_bf16), "cst_ws": Wm.imag.astype(_bf16),
        "cst_wcn": (-Wm.real).astype(_bf16), "cst_wsn": (-Wm.imag).astype(_bf16),
        "cst_twr": TW.real.astype(np.float32), "cst_twi": TW.imag.astype(np.float32),
    }


def decl_ssm_consts(nc):
    d = {}
    for n in ["cst_wc", "cst_ws", "cst_wcn", "cst_wsn"]:
        d[n] = nc.dram_tensor(n, [128, 128], BF16, kind="ExternalInput").ap()
    for n in ["cst_twr", "cst_twi"]:
        d[n] = nc.dram_tensor(n, [128, 128], F32, kind="ExternalInput").ap()
    return d


def emit_ssm_phaseU(nc, block, s, cst, x_in, W, uln_hbm):
    with ExitStack() as _es:
        _e = _es.enter_context
        xb = _e(_sb(nc, "U_xb", [128, 2, 1024], F32))
        sq = _e(_sb(nc, "U_sq", [128, 1024], F32))
        st16 = _e(_sb(nc, "U_st", [128, 32], F32))
        xnf = _e(_sb(nc, "U_xnf", [128, 1024], F32))
        xnb = _e(_sb(nc, "U_xnb", [128, 2, 1024], BF16))
        xnfm = _e(_sb(nc, "U_xnfm", [128, 2, 8 * 512], BF16))
        eyebf = _e(_sb(nc, "U_eyebf", [128, 128], BF16))
        eyef = _e(_sb(nc, "U_eyef", [128, 128], F32))
        winf = _e(_sb(nc, "U_winf", [128, 8 * 1024], BF16))
        wint = _e(_sb(nc, "U_wint", [128, 8 * 1024], BF16))
        bin_ = _e(_sb(nc, "U_bin", [128, 8], F32))
        gb = _e(_sb(nc, "U_gb", [128, 2 * 1024], F32))
        grow = _e(_sb(nc, "U_grow", [1, 2 * 1024], F32))
        ronep = _e(_sb(nc, "U_onep", [1, 128], F32))
        ufm = _e(_sb(nc, "U_ufm", [128, 8 * 512], F32))
        utm = _e(_sb(nc, "U_utm", [128, 1024], F32))
        uo = _e(_sb(nc, "U_uo", [128, 2, 4 * 1024], F32))
        psT = _e(_ps(nc, "U_psT", [128, 1024], BF16))
        psU = _e(_ps(nc, "U_psU", [128, 3, 512], F32))
        psV = _e(_ps(nc, "U_psV", [128, 2, 1024], F32))

        @block.sync
        def _(q):
            s.inc(q.dma_start(out=eyebf[:], in_=cst["cst_eye_bf"][:, :]), "qx", 16)
            s.inc(q.dma_start(out=eyef[:], in_=cst["cst_eye_f32"][:, :]), "qx", 16)
            s.inc(q.dma_start(out=grow[0:1, 0:1024], in_=W["gamma"][None, :]), "qx", 16)
            s.inc(q.dma_start(out=grow[0:1, 1024:2048], in_=W["beta"][None, :]), "qx", 16)
            s.inc(q.dma_start(out=ronep[0:1, :], in_=cst["cst_oneh"][0:1, 0:128]), "qx", 16)
            s.inc(q.dma_start(out=bin_[:, :], in_=W["b_in"].rearrange("(c p) -> p c", p=128),
                              allow_slow_non_contiguous=True), "qx", 16)
        @block.gpsimd
        def _(gq):
            for k in range(DC):
                s.inc(gq.dma_start(out=winf[:, k * 1024:(k + 1) * 1024],
                                   in_=W["w_in"].rearrange("(k p) d -> k p d", p=128)[k]), "qw", 16)
        prep_qx = s.c["qx"]; prep_qw = s.c["qw"]
        @block.tensor
        def _(tn):
            tn.wait_ge(s.h["qx"], prep_qx)
            for hh in range(2):
                tn.matmul(psV[:, 0, hh * 512:(hh + 1) * 512], ronep[0:1, :],
                          grow[0:1, hh * 512:(hh + 1) * 512], start=True, stop=True)
                i = tn.matmul(psV[:, 1, hh * 512:(hh + 1) * 512], ronep[0:1, :],
                              grow[0:1, 1024 + hh * 512:1024 + (hh + 1) * 512], start=True, stop=True)
            s.inc(i, "tn")
        gb_c = s.c["tn"]
        @block.vector
        def _(v):
            v.wait_ge(s.h["tn"], gb_c)
            v.tensor_copy(gb[:, 0:1024], psV[:, 0, :])
            s.inc(v.tensor_copy(gb[:, 1024:2048], psV[:, 1, :]), "ve")
        gbv_c = s.c["ve"]
        for cc in range(DC):
            @block.tensor
            def _(tn):
                tn.wait_ge(s.h["qw"], prep_qw)
                if cc == 0:
                    tn.wait_ge(s.h["ve"], gbv_c)
                else:
                    tn.wait_ge(s.h["ve"], s.c["ve"])
                for kk in range(DC):
                    i = tn.transpose(psT[:, kk * 128:(kk + 1) * 128],
                                     winf[:, cc * 1024 + kk * 128: cc * 1024 + kk * 128 + 128], eyebf[:])
                s.inc(i, "tn")
            wt_c = s.c["tn"]
            @block.vector
            def _(v):
                v.wait_ge(s.h["tn"], wt_c)
                s.inc(v.tensor_copy(
                    wint[:].rearrange("p (k e) -> p k e", k=DC)[:, :, cc * 128:(cc + 1) * 128],
                    psT[:].rearrange("p (k e) -> p k e", k=DC)), "ve")
        wt_done_ve = s.c["ve"]
        s.c["spsT_used"] = wt_done_ve

        for g in range(16):
            gslot = g % 2
            for ti in range(4):
                t = g * 4 + ti
                tslot = t % 2
                @block.sync
                def _(q):
                    q.wait_ge(s.h["ve"], s.get(f"sxb_used_{tslot}"))
                    s.inc(q.dma_start(out=xb[:, tslot, :], in_=x_in[t * 128:(t + 1) * 128, :]), "qx", 16)
                ld_c = s.c["qx"]
                emit_ln_tile(block, s, xb[:, tslot, :], xnf[:], sq,
                             st16[:, tslot * 16:(tslot + 1) * 16], ld_sem="qx", ld_cnt=ld_c)
                s.c[f"sxb_used_{tslot}"] = s.c["ve"]
                @block.vector
                def _(v):
                    v.wait_ge(s.h["tn"], s.get(f"sxnb_used_{tslot}"))
                    s.inc(v.tensor_copy(xnb[:, tslot, :], xnf[:]), "ve")
                xn_c = s.c["ve"]
                @block.tensor
                def _(tn):
                    tn.wait_ge(s.h["ve"], xn_c)
                    tn.wait_ge(s.h["ve"], s.get("spsT_used"))
                    for c in range(DC):
                        i = tn.transpose(psT[:, c * 128:(c + 1) * 128],
                                         xnb[:, tslot, c * 128:(c + 1) * 128], eyebf[:])
                    s.inc(i, "tn")
                s.c[f"sxnb_used_{tslot}"] = s.c["tn"]
                tr_c = s.c["tn"]
                @block.vector
                def _(v):
                    v.wait_ge(s.h["tn"], tr_c)
                    if ti == 0:
                        v.wait_ge(s.h["tn"], s.get(f"sxnfm_used_{gslot}"))
                    s.inc(v.tensor_copy(
                        xnfm[:, gslot, :].rearrange("p (c l) -> p c l", c=DC)[:, :, ti * 128:(ti + 1) * 128],
                        psT[:].rearrange("p (c l) -> p c l", c=DC)), "ve")
                    s.c["spsT_used"] = s.c["ve"]
            xng_c = s.c["ve"]
            u_ac_base = s.c["ac"]
            @block.tensor
            def _(tn):
                tn.wait_ge(s.h["ve"], xng_c)
                for m in range(DC):
                    if m >= 3:
                        tn.wait_ge(s.h["ac"], u_ac_base + m - 2)
                    for k in range(DC):
                        i = tn.matmul(psU[:, m % 3, :], wint[:, k * 1024 + m * 128: k * 1024 + m * 128 + 128],
                                      xnfm[:, gslot, k * 512:(k + 1) * 512],
                                      start=(k == 0), stop=(k == DC - 1))
                    s.inc(i, "tn")
                s.c[f"sxnfm_used_{gslot}"] = s.c["tn"]
            umm_base = s.c["tn"] - DC
            @block.scalar
            def _(a):
                a.wait_ge(s.h["tn"], s.get("ufm_used"))
                for m in range(DC):
                    a.wait_ge(s.h["tn"], umm_base + m + 1)
                    s.inc(a.activation(ufm[:, m * 512:(m + 1) * 512], psU[:, m % 3, :],
                                       AF.Identity, bias=bin_[:, m:m + 1]), "ac")
            u_done = s.c["ac"]
            for ti in range(4):
                pslot = ti % 2
                @block.tensor
                def _(tn):
                    tn.wait_ge(s.h["ac"], u_done)
                    tn.wait_ge(s.h["ve"], s.get(f"spsV_used_{pslot}"))
                    for c in range(DC):
                        i = tn.transpose(psV[:, pslot, c * 128:(c + 1) * 128],
                                         ufm[:].rearrange("p (c l) -> p c l", c=DC)[:, c, ti * 128:(ti + 1) * 128],
                                         eyef[:])
                    s.inc(i, "tn")
                if ti == 3:
                    s.c["ufm_used"] = s.c["tn"]
                ut_c = s.c["tn"]
                emit_ln_tile(block, s, psV[:, pslot, :], utm[:], sq,
                             st16[:, pslot * 16:(pslot + 1) * 16], ld_sem="tn", ld_cnt=ut_c)
                @block.vector
                def _(v):
                    if ti == 0:
                        v.wait_ge(s.h["qs"], s.get(f"uo_st_{gslot}"))
                    s.edge(v, v.tensor_tensor(utm[:], utm[:], gb[:, 0:1024], OP.mult))
                    s.inc(v.tensor_tensor(uo[:, gslot, ti * 1024:(ti + 1) * 1024], utm[:],
                                          gb[:, 1024:2048], OP.add), "ve")
                    s.c[f"spsV_used_{pslot}"] = s.c["ve"]
            uo_c = s.c["ve"]
            @block.scalar
            def _(a):
                a.wait_ge(s.h["ve"], uo_c)
                s.inc(a.dma_start(out=uln_hbm[g * 512:(g + 1) * 512, :].rearrange("(ti p) f -> p ti f", p=128),
                                  in_=uo[:, gslot, :].rearrange("p (ti f) -> p ti f", ti=4)), "qs", 16)
                s.c[f"uo_st_{gslot}"] = s.c["qs"]
    phase_barrier(block, s)


def emit_ssm_phaseK(nc, block, s, cst, W, kk_hbm):
    with ExitStack() as _es:
        _e = _es.enter_context
        pr = _e(_sb(nc, "K_pr", [65, 8192], F32))
        pi = _e(_sb(nc, "K_pi", [65, 8192], F32))
        tmpa = _e(_sb(nc, "K_tmpa", [65, 8192], F32))
        ab = _e(_sb(nc, "K_ab", [64, 8], F32))
        cscr = _e(_sb(nc, "K_cscr", [128, 512], F32))
        gt = _e(_sb(nc, "K_gt", [65, 2 * 1024], F32))
        colj = _e(_sb(nc, "K_colj", [64, 8], F32))
        ko = _e(_sb(nc, "K_ko", [128, 2, 1024], F32))
        eyef = _e(_sb(nc, "K_eyef", [128, 128], F32))
        psK = _e(_ps(nc, "K_psK", [128, 2, 2, 512], F32))
        psC = _e(_ps(nc, "K_psC", [64, 1024], F32))

        @block.sync
        def _(q):
            s.inc(q.dma_start(out=eyef[:], in_=cst["cst_eye_f32"][:, :]), "qx", 16)
            s.inc(q.dma_start(out=ab[:, 0], in_=W["A_re"][:]), "qx", 16)
            s.inc(q.dma_start(out=ab[:, 1], in_=W["A_im"][:]), "qx", 16)
            s.inc(q.dma_start(out=ab[:, 2], in_=W["B_re"][:]), "qx", 16)
            s.inc(q.dma_start(out=ab[:, 3], in_=W["B_im"][:]), "qx", 16)
            s.inc(q.dma_start(out=gt[64:65, 0:1024], in_=W["D_re"][None, :]), "qx", 16)
            s.inc(q.dma_start(out=cscr[0:1, 0:512], in_=W["D_im"][None, 0:512]), "qx", 16)
        kprep_qx = s.c["qx"]
        for half, wname in [(0, "C_re"), (1, "C_im")]:
            @block.sync
            def _(q):
                q.wait_ge(s.h["ve"], s.c["ve"])
                s.inc(q.dma_start(out=cscr[:].rearrange("p (c n) -> p c n", n=64),
                                  in_=W[wname].rearrange("(c p) n -> p c n", p=128)), "qx", 16)
            ch_c = s.c["qx"]
            @block.tensor
            def _(tn):
                tn.wait_ge(s.h["qx"], ch_c)
                tn.wait_ge(s.h["qx"], kprep_qx)
                if half == 1:
                    tn.wait_ge(s.h["ve"], s.c["ve"])
                for c in range(DC):
                    i = tn.transpose(psC[0:64, c * 128:(c + 1) * 128],
                                     cscr[:, c * 64:(c + 1) * 64], eyef[:])
                s.inc(i, "tn")
            ct_c = s.c["tn"]
            @block.vector
            def _(v):
                v.wait_ge(s.h["tn"], ct_c)
                s.inc(v.tensor_copy(tmpa[0:64, half * 1024:(half + 1) * 1024], psC[0:64, :]), "ve")
        ct_done = s.c["ve"]
        @block.vector
        def _(v):
            v.wait_ge(s.h["ve"], ct_done)
            s.edge(v, v.tensor_scalar_mul(tmpa[0:64, 2048:3072], tmpa[0:64, 0:1024], ab[:, 2:3]))
            s.edge(v, v.tensor_scalar_mul(tmpa[0:64, 3072:4096], tmpa[0:64, 1024:2048], ab[:, 3:4]))
            s.edge(v, v.tensor_scalar_mul(tmpa[0:64, 4096:5120], tmpa[0:64, 0:1024], ab[:, 3:4]))
            s.edge(v, v.tensor_scalar_mul(tmpa[0:64, 5120:6144], tmpa[0:64, 1024:2048], ab[:, 2:3]))
            s.edge(v, v.tensor_tensor(gt[0:64, 0:1024], tmpa[0:64, 2048:3072], tmpa[0:64, 3072:4096], OP.subtract))
            s.edge(v, v.tensor_tensor(gt[0:64, 1024:2048], tmpa[0:64, 4096:5120], tmpa[0:64, 5120:6144], OP.add))
            s.edge(v, v.tensor_scalar_mul(gt[0:64, 1024:2048], gt[0:64, 1024:2048], -1.0))
            s.edge(v, v.memset(gt[64:65, 1024:2048], 0.0))
            s.edge(v, v.memset(pr[64:65, :], 1.0))
            s.edge(v, v.memset(pi[64:65, :], 0.0))
            s.edge(v, v.memset(pr[0:64, 0:1], 1.0))
            s.edge(v, v.memset(pi[0:64, 0:1], 0.0))
            J = 1
            while J < 8192:
                s.edge(v, v.tensor_scalar_mul(colj[:, 2:3], pi[0:64, J - 1:J], ab[:, 1:2]))
                s.edge(v, v.tensor_scalar_mul(colj[:, 3:4], pi[0:64, J - 1:J], ab[:, 0:1]))
                s.edge(v, v.scalar_tensor_tensor(colj[:, 0:1], pr[0:64, J - 1:J], ab[:, 0:1],
                                                 colj[:, 2:3], OP.mult, OP.subtract))
                s.edge(v, v.scalar_tensor_tensor(colj[:, 1:2], pr[0:64, J - 1:J], ab[:, 1:2],
                                                 colj[:, 3:4], OP.mult, OP.add))
                JJ = min(J, 8192 - J)
                s.edge(v, v.tensor_scalar_mul(tmpa[0:64, 0:JJ], pi[0:64, 0:JJ], colj[:, 1:2]))
                s.edge(v, v.tensor_scalar_mul(tmpa[0:64, 4096:4096 + JJ], pi[0:64, 0:JJ], colj[:, 0:1]))
                s.edge(v, v.scalar_tensor_tensor(pr[0:64, J:J + JJ], pr[0:64, 0:JJ], colj[:, 0:1],
                                                 tmpa[0:64, 0:JJ], OP.mult, OP.subtract))
                s.edge(v, v.scalar_tensor_tensor(pi[0:64, J:J + JJ], pr[0:64, 0:JJ], colj[:, 1:2],
                                                 tmpa[0:64, 4096:4096 + JJ], OP.mult, OP.add))
                J *= 2
            s.inc(v.tensor_copy(colj[:, 4:5], pr[0:64, 0:1]), "ve")
        p_done = s.c["ve"]
        for jt in range(NT):
            @block.tensor
            def _(tn):
                tn.wait_ge(s.h["ve"], p_done)
                if jt >= 2:
                    tn.wait_ge(s.h["ve"], s.get(f"kps_used_{jt % 2}"))
                for hh in range(2):
                    tn.matmul(psK[:, jt % 2, hh, :],
                              pr[:, jt * 128:(jt + 1) * 128], gt[:, hh * 512:(hh + 1) * 512],
                              start=True, stop=False)
                    i = tn.matmul(psK[:, jt % 2, hh, :],
                                  pi[:, jt * 128:(jt + 1) * 128], gt[:, 1024 + hh * 512:1024 + (hh + 1) * 512],
                                  start=False, stop=True)
                s.inc(i, "tn")
            kmm_c = s.c["tn"]
            @block.vector
            def _(v):
                v.wait_ge(s.h["tn"], kmm_c)
                v.wait_ge(s.h["qs"], s.get(f"ko_st_{jt % 2}"))
                s.inc(v.tensor_copy(ko[:, jt % 2, :], psK[:, jt % 2, :, :].rearrange("p h l -> p (h l)")), "ve")
                s.c[f"kps_used_{jt % 2}"] = s.c["ve"]
            kev_c = s.c["ve"]
            @block.scalar
            def _(a):
                a.wait_ge(s.h["ve"], kev_c)
                s.inc(a.dma_start(out=kk_hbm[jt * 128:(jt + 1) * 128, :], in_=ko[:, jt % 2, :]), "qs", 16)
                s.c[f"ko_st_{jt % 2}"] = s.c["qs"]
    phase_barrier(block, s)


def emit_ssm_fft(nc, block, s, cst, scst, x_in, x_out, uln_hbm, kk_hbm, g1_hbm, sp_hbm):
    N_INV = 1.0 / 16384.0
    with ExitStack() as _es:
        _e = _es.enter_context
        wc = _e(_sb(nc, "F_wc", [128, 128], BF16))
        ws = _e(_sb(nc, "F_ws", [128, 128], BF16))
        us = _e(_sb(nc, "F_us", [64, 2, 1024], BF16))
        ks = _e(_sb(nc, "F_ks", [64, 2, 1024], BF16))
        ev = _e(_sb(nc, "F_ev", [128, 2, 4 * 1024], BF16))
        psF = _e(_ps(nc, "F_psF", [128, 8, 512], F32))
        @block.sync
        def _(q):
            s.inc(q.dma_start(out=wc[:], in_=scst["cst_wc"][:, :]), "qx", 16)
            s.inc(q.dma_start(out=ws[:], in_=scst["cst_ws"][:, :]), "qx", 16)
        f1p_c = s.c["qx"]
        for n1 in range(128):
            slot = n1 % 2
            @block.gpsimd
            def _(gq):
                gq.wait_ge(s.h["tn"], s.get(f"f1in_used_{slot}"))
                s.inc(gq.dma_start(out=us[:, slot, :], in_=uln_hbm[n1::128, :]), "qw", 16)
                s.inc(gq.dma_start(out=ks[:, slot, :], in_=kk_hbm[n1::128, :]), "qw", 16)
            l_c = s.c["qw"]
            @block.tensor
            def _(tn):
                tn.wait_ge(s.h["qw"], l_c)
                tn.wait_ge(s.h["qx"], f1p_c)
                tn.wait_ge(s.h["ve"], s.get("f1ps_used"))
                for (srcb, base) in [(us, 0), (ks, 4)]:
                    for hh in range(2):
                        tn.matmul(psF[:, base + hh, :], wc[0:64, :],
                                  srcb[:, slot, hh * 512:(hh + 1) * 512], start=True, stop=True)
                        i = tn.matmul(psF[:, base + 2 + hh, :], ws[0:64, :],
                                      srcb[:, slot, hh * 512:(hh + 1) * 512], start=True, stop=True)
                s.inc(i, "tn")
                s.c[f"f1in_used_{slot}"] = s.c["tn"]
            mm_c = s.c["tn"]
            @block.vector
            def _(v):
                v.wait_ge(s.h["tn"], mm_c)
                v.wait_ge(s.h["qs"], s.get(f"f1ev_st_{slot}"))
                s.inc(v.tensor_copy(ev[:, slot, :], psF[:].rearrange("p b l -> p (b l)")), "ve")
                s.c["f1ps_used"] = s.c["ve"]
            e_c = s.c["ve"]
            @block.scalar
            def _(a):
                a.wait_ge(s.h["ve"], e_c)
                s.inc(a.dma_start(out=g1_hbm[:, :, :, n1, :].rearrange("w r k l -> k w r l"),
                                  in_=ev[:, slot, :].rearrange("p (w r l) -> p w r l", w=2, r=2)), "qs", 16)
                s.c[f"f1ev_st_{slot}"] = s.c["qs"]
    phase_barrier(block, s)

    with ExitStack() as _es:
        _e = _es.enter_context
        wc = _e(_sb(nc, "G_wc", [128, 128], BF16))
        ws = _e(_sb(nc, "G_ws", [128, 128], BF16))
        twr = _e(_sb(nc, "G_twr", [128, 128], F32))
        twi = _e(_sb(nc, "G_twi", [128, 128], F32))
        wpr = _e(_sb(nc, "G_wpr", [128, 128 * 128], BF16))
        wpi = _e(_sb(nc, "G_wpi", [128, 128 * 128], BF16))
        wtmp = _e(_sb(nc, "G_wtmp", [128, 2 * 128], BF16))
        ld = _e(_sb(nc, "G_ld", [128, 2, 4 * 1024], BF16))
        ldn = _e(_sb(nc, "G_ldn", [128, 2, 2 * 1024], BF16))
        ev = _e(_sb(nc, "G_ev", [128, 2, 4 * 1024], BF16))
        psF = _e(_ps(nc, "G_psF", [128, 8, 512], F32))
        @block.sync
        def _(q):
            s.inc(q.dma_start(out=wc[:], in_=scst["cst_wc"][:, :]), "qx", 16)
            s.inc(q.dma_start(out=ws[:], in_=scst["cst_ws"][:, :]), "qx", 16)
            s.inc(q.dma_start(out=twr[:], in_=scst["cst_twr"][:, :]), "qx", 16)
            s.inc(q.dma_start(out=twi[:], in_=scst["cst_twi"][:, :]), "qx", 16)
        f2p_c = s.c["qx"]
        @block.vector
        def _(v):
            v.wait_ge(s.h["qx"], f2p_c)
            for k2 in range(128):
                s.edge(v, v.tensor_scalar_mul(wtmp[:, 0:128], wc[:], twr[:, k2:k2 + 1]))
                s.edge(v, v.tensor_scalar_mul(wtmp[:, 128:256], ws[:], twi[:, k2:k2 + 1]))
                s.edge(v, v.tensor_tensor(wpr[:, k2 * 128:(k2 + 1) * 128], wtmp[:, 0:128],
                                          wtmp[:, 128:256], OP.subtract))
                s.edge(v, v.tensor_scalar_mul(wtmp[:, 0:128], wc[:], twi[:, k2:k2 + 1]))
                s.edge(v, v.tensor_scalar_mul(wtmp[:, 128:256], ws[:], twr[:, k2:k2 + 1]))
                s.edge(v, v.tensor_tensor(wpi[:, k2 * 128:(k2 + 1) * 128], wtmp[:, 0:128],
                                          wtmp[:, 128:256], OP.add))
            s.inc(v.tensor_copy(wtmp[:, 0:128], wc[:]), "ve")
        wp_done = s.c["ve"]
        for k2 in range(128):
            slot = k2 % 2
            @block.sync
            def _(q):
                q.wait_ge(s.h["tn"], s.get(f"f2in_used_{slot}"))
                s.inc(q.dma_start(out=ld[:, slot, :].rearrange("p (w r l) -> p w r l", w=2, r=2),
                                  in_=g1_hbm[:, :, k2, :, :].rearrange("w r p l -> p w r l")), "qx", 16)
            l_c = s.c["qx"]
            @block.vector
            def _(v):
                v.wait_ge(s.h["qx"], l_c)
                v.wait_ge(s.h["ve"], wp_done)
                v.wait_ge(s.h["tn"], s.get(f"f2ldn_used_{slot}"))
                v.tensor_scalar_mul(ldn[:, slot, 0:1024], ld[:, slot, 1024:2048], -1.0)
                s.inc(v.tensor_scalar_mul(ldn[:, slot, 1024:2048], ld[:, slot, 3072:4096], -1.0), "ve")
            n_c = s.c["ve"]
            @block.tensor
            def _(tn):
                tn.wait_ge(s.h["ve"], n_c)
                tn.wait_ge(s.h["ve"], s.get("f2ps_used"))
                wl = wpr[:, k2 * 128:(k2 + 1) * 128]
                wi_ = wpi[:, k2 * 128:(k2 + 1) * 128]
                for wbase, rbase in [(0, 0), (4, 2048)]:
                    for hh in range(2):
                        tn.matmul(psF[:, wbase + hh, :], wl,
                                  ld[:, slot, rbase + hh * 512: rbase + (hh + 1) * 512], start=True, stop=False)
                        tn.matmul(psF[:, wbase + hh, :], wi_,
                                  ldn[:, slot, (rbase // 2) + hh * 512:(rbase // 2) + (hh + 1) * 512],
                                  start=False, stop=True)
                        tn.matmul(psF[:, wbase + 2 + hh, :], wl,
                                  ld[:, slot, rbase + 1024 + hh * 512: rbase + 1024 + (hh + 1) * 512],
                                  start=True, stop=False)
                        i = tn.matmul(psF[:, wbase + 2 + hh, :], wi_,
                                      ld[:, slot, rbase + hh * 512: rbase + (hh + 1) * 512],
                                      start=False, stop=True)
                s.inc(i, "tn")
                s.c[f"f2in_used_{slot}"] = s.c["tn"]
                s.c[f"f2ldn_used_{slot}"] = s.c["tn"]
            mm_c = s.c["tn"]
            @block.vector
            def _(v):
                v.wait_ge(s.h["tn"], mm_c)
                v.wait_ge(s.h["qs"], s.get(f"f2ev_st_{slot}"))
                s.inc(v.tensor_copy(ev[:, slot, :], psF[:].rearrange("p b l -> p (b l)")), "ve")
                s.c["f2ps_used"] = s.c["ve"]
            e_c = s.c["ve"]
            @block.scalar
            def _(a):
                a.wait_ge(s.h["ve"], e_c)
                s.inc(a.dma_start(out=sp_hbm[:, :, k2, :, :].rearrange("w r p l -> p w r l"),
                                  in_=ev[:, slot, :].rearrange("p (w r l) -> p w r l", w=2, r=2)), "qs", 16)
                s.c[f"f2ev_st_{slot}"] = s.c["qs"]
    phase_barrier(block, s)

    with ExitStack() as _es:
        _e = _es.enter_context
        wc = _e(_sb(nc, "P_wc", [128, 128], BF16))
        ws = _e(_sb(nc, "P_ws", [128, 128], BF16))
        wsn = _e(_sb(nc, "P_wsn", [128, 128], BF16))
        ld = _e(_sb(nc, "P_ld", [128, 2, 4 * 1024], BF16))
        pw = _e(_sb(nc, "P_pw", [128, 2, 2 * 1024], BF16))
        pt = _e(_sb(nc, "P_pt", [128, 4 * 1024], BF16))
        ev = _e(_sb(nc, "P_ev", [128, 2, 2 * 1024], BF16))
        psF = _e(_ps(nc, "P_psF", [128, 8, 512], F32))
        @block.sync
        def _(q):
            s.inc(q.dma_start(out=wc[:], in_=scst["cst_wc"][:, :]), "qx", 16)
            s.inc(q.dma_start(out=ws[:], in_=scst["cst_ws"][:, :]), "qx", 16)
            s.inc(q.dma_start(out=wsn[:], in_=scst["cst_wsn"][:, :]), "qx", 16)
        pip_c = s.c["qx"]
        for k2 in range(128):
            slot = k2 % 2
            @block.sync
            def _(q):
                q.wait_ge(s.h["po"], s.get(f"piin_used_{slot}"))
                s.inc(q.dma_start(out=ld[:, slot, :].rearrange("p (w r l) -> p w r l", w=2, r=2),
                                  in_=sp_hbm[:, :, k2, :, :].rearrange("w r p l -> p w r l")), "qx", 16)
            l_c = s.c["qx"]
            @block.gpsimd
            def _(g):
                g.wait_ge(s.h["qx"], l_c)
                g.wait_ge(s.h["tn"], s.get(f"pipw_used_{slot}"))
                xr = ld[:, slot, 0:1024]; xi = ld[:, slot, 1024:2048]
                kr = ld[:, slot, 2048:3072]; ki = ld[:, slot, 3072:4096]
                s.edge(g, g.tensor_tensor(pt[:, 0:1024], xr, kr, OP.mult), "pos")
                s.edge(g, g.tensor_tensor(pt[:, 1024:2048], xi, ki, OP.mult), "pos")
                s.edge(g, g.tensor_tensor(pt[:, 2048:3072], xr, ki, OP.mult), "pos")
                s.edge(g, g.tensor_tensor(pt[:, 3072:4096], xi, kr, OP.mult), "pos")
                s.edge(g, g.tensor_tensor(pw[:, slot, 0:1024], pt[:, 0:1024], pt[:, 1024:2048], OP.subtract), "pos")
                i = g.tensor_tensor(pw[:, slot, 1024:2048], pt[:, 2048:3072], pt[:, 3072:4096], OP.add)
                s.inc(i, "po")
                s.c[f"piin_used_{slot}"] = s.c["po"]
            p_c = s.c["po"]
            ps = psF[:, 0:4, :] if slot == 0 else psF[:, 4:8, :]
            @block.tensor
            def _(tn):
                tn.wait_ge(s.h["po"], p_c)
                tn.wait_ge(s.h["qx"], pip_c)
                tn.wait_ge(s.h["ve"], s.get(f"pips_used_{slot}"))
                pr_ = pw[:, slot, 0:1024]; pi_ = pw[:, slot, 1024:2048]
                for hh in range(2):
                    tn.matmul(ps[:, hh, :], wc[:], pr_[:, hh * 512:(hh + 1) * 512], start=True, stop=False)
                    tn.matmul(ps[:, hh, :], ws[:], pi_[:, hh * 512:(hh + 1) * 512], start=False, stop=True)
                    tn.matmul(ps[:, 2 + hh, :], wsn[:], pr_[:, hh * 512:(hh + 1) * 512], start=True, stop=False)
                    i = tn.matmul(ps[:, 2 + hh, :], wc[:], pi_[:, hh * 512:(hh + 1) * 512], start=False, stop=True)
                s.inc(i, "tn")
                s.c[f"pipw_used_{slot}"] = s.c["tn"]
            mm_c = s.c["tn"]
            @block.vector
            def _(v):
                v.wait_ge(s.h["tn"], mm_c)
                v.wait_ge(s.h["qs"], s.get(f"piev_st_{slot}"))
                s.inc(v.tensor_copy(ev[:, slot, :], ps[:, 0:4, :].rearrange("p b l -> p (b l)")), "ve")
                s.c[f"pips_used_{slot}"] = s.c["ve"]
            e_c = s.c["ve"]
            @block.scalar
            def _(a):
                a.wait_ge(s.h["ve"], e_c)
                s.inc(a.dma_start(out=g1_hbm[0, :, :, k2, :].rearrange("r p l -> p r l"),
                                  in_=ev[:, slot, :].rearrange("p (r l) -> p r l", r=2)), "qs", 16)
                s.c[f"piev_st_{slot}"] = s.c["qs"]
    phase_barrier(block, s)

    with ExitStack() as _es:
        _e = _es.enter_context
        twr = _e(_sb(nc, "Z_twr", [128, 128], F32))
        twi = _e(_sb(nc, "Z_twi", [128, 128], F32))
        wc = _e(_sb(nc, "Z_wc", [128, 128], BF16))
        ws = _e(_sb(nc, "Z_ws", [128, 128], BF16))
        wpr = _e(_sb(nc, "Z_wpr", [128, 128 * 128], BF16))
        wpi = _e(_sb(nc, "Z_wpi", [128, 128 * 128], BF16))
        wtmp = _e(_sb(nc, "Z_wtmp", [128, 2 * 128], BF16))
        ld = _e(_sb(nc, "Z_ld", [128, 2, 2 * 1024], BF16))
        xs = _e(_sb(nc, "Z_xs", [64, 2, 2 * 1024], F32))
        xo = _e(_sb(nc, "Z_xo", [64, 2, 1024], F32))
        psF = _e(_ps(nc, "Z_psF", [64, 4, 512], F32))
        @block.sync
        def _(q):
            s.inc(q.dma_start(out=twr[:], in_=scst["cst_twr"][:, :]), "qx", 16)
            s.inc(q.dma_start(out=twi[:], in_=scst["cst_twi"][:, :]), "qx", 16)
            s.inc(q.dma_start(out=wc[:], in_=scst["cst_wc"][:, :]), "qx", 16)
            s.inc(q.dma_start(out=ws[:], in_=scst["cst_ws"][:, :]), "qx", 16)
        zp_c = s.c["qx"]
        @block.vector
        def _(v):
            v.wait_ge(s.h["qx"], zp_c)
            for z2 in range(128):
                s.edge(v, v.tensor_scalar_mul(wtmp[:, 0:128], wc[:], twr[:, z2:z2 + 1]))
                s.edge(v, v.tensor_scalar_mul(wtmp[:, 128:256], ws[:], twi[:, z2:z2 + 1]))
                s.edge(v, v.tensor_tensor(wpr[:, z2 * 128:(z2 + 1) * 128], wtmp[:, 0:128],
                                          wtmp[:, 128:256], OP.subtract))
                s.edge(v, v.tensor_scalar_mul(wtmp[:, 0:128], wc[:], twi[:, z2:z2 + 1]))
                s.edge(v, v.tensor_scalar_mul(wtmp[:, 128:256], ws[:], twr[:, z2:z2 + 1]))
                s.edge(v, v.tensor_tensor(wpi[:, z2 * 128:(z2 + 1) * 128], wtmp[:, 0:128],
                                          wtmp[:, 128:256], OP.add))
            s.inc(v.tensor_copy(wtmp[:, 0:128], wc[:]), "ve")
        wp_done = s.c["ve"]
        for z2 in range(128):
            slot = z2 % 2
            @block.sync
            def _(q):
                q.wait_ge(s.h["tn"], s.get(f"zin_used_{slot}"))
                s.inc(q.dma_start(out=ld[:, slot, :].rearrange("p (r l) -> p r l", r=2),
                                  in_=g1_hbm[0, :, z2, :, :].rearrange("r p l -> p r l")), "qx", 16)
                q.wait_ge(s.h["ve"], s.get(f"zxs_used_{slot}"))
                s.inc(q.dma_start(out=xs[:, slot, 0:1024], in_=x_in[z2::128, :]), "qx", 16)
                s.inc(q.dma_start(out=xs[:, slot, 1024:2048], in_=uln_hbm[z2::128, :]), "qx", 16)
            l_c = s.c["qx"]
            ps = psF[:, 0:2, :] if slot == 0 else psF[:, 2:4, :]
            @block.tensor
            def _(tn):
                tn.wait_ge(s.h["qx"], l_c)
                tn.wait_ge(s.h["ve"], wp_done)
                tn.wait_ge(s.h["ve"], s.get(f"zps_used_{slot}"))
                wl = wpr[:, z2 * 128:z2 * 128 + 64]
                wi_ = wpi[:, z2 * 128:z2 * 128 + 64]
                for hh in range(2):
                    tn.matmul(ps[:, hh, :], wl, ld[:, slot, hh * 512:(hh + 1) * 512], start=True, stop=False)
                    i = tn.matmul(ps[:, hh, :], wi_, ld[:, slot, 1024 + hh * 512:1024 + (hh + 1) * 512],
                                  start=False, stop=True)
                s.inc(i, "tn")
                s.c[f"zin_used_{slot}"] = s.c["tn"]
            mm_c = s.c["tn"]
            @block.vector
            def _(v):
                v.wait_ge(s.h["tn"], mm_c)
                v.wait_ge(s.h["qs"], s.get(f"zxo_st_{slot}"))
                s.edge(v, v.scalar_tensor_tensor(xo[:, slot, :], ps.rearrange("p b l -> p (b l)"),
                                                 N_INV, xs[:, slot, 0:1024], OP.mult, OP.add))
                s.inc(v.tensor_tensor(xo[:, slot, :], xo[:, slot, :], xs[:, slot, 1024:2048], OP.add), "ve")
                s.c[f"zps_used_{slot}"] = s.c["ve"]
                s.c[f"zxs_used_{slot}"] = s.c["ve"]
            e_c = s.c["ve"]
            @block.scalar
            def _(a):
                a.wait_ge(s.h["ve"], e_c)
                s.inc(a.dma_start(out=x_out[z2::128, :], in_=xo[:, slot, :]), "qs", 16)
                s.c[f"zxo_st_{slot}"] = s.c["qs"]
    phase_barrier(block, s)


def ssm_weight_decls(nc, pfx=""):
    return {
        "w_in": nc.dram_tensor(pfx + "W_in", [D, D], F32, kind="ExternalInput").ap(),
        "b_in": nc.dram_tensor(pfx + "b_in", [D], F32, kind="ExternalInput").ap(),
        "gamma": nc.dram_tensor(pfx + "gamma", [D], F32, kind="ExternalInput").ap(),
        "beta": nc.dram_tensor(pfx + "beta", [D], F32, kind="ExternalInput").ap(),
        "A_re": nc.dram_tensor(pfx + "A_re", [64], F32, kind="ExternalInput").ap(),
        "A_im": nc.dram_tensor(pfx + "A_im", [64], F32, kind="ExternalInput").ap(),
        "B_re": nc.dram_tensor(pfx + "B_re", [64], F32, kind="ExternalInput").ap(),
        "B_im": nc.dram_tensor(pfx + "B_im", [64], F32, kind="ExternalInput").ap(),
        "C_re": nc.dram_tensor(pfx + "C_re", [D, 64], F32, kind="ExternalInput").ap(),
        "C_im": nc.dram_tensor(pfx + "C_im", [D, 64], F32, kind="ExternalInput").ap(),
        "D_re": nc.dram_tensor(pfx + "D_re", [D], F32, kind="ExternalInput").ap(),
        "D_im": nc.dram_tensor(pfx + "D_im", [D], F32, kind="ExternalInput").ap(),
    }


def ssm_weight_inmap(inputs):
    return {
        "W_in": inputs["L0_attn_W_in"], "b_in": inputs["L0_attn_b_in"],
        "gamma": inputs["L0_attn_gamma"], "beta": inputs["L0_attn_beta"],
        "A_re": inputs["L0_attn_A_re"], "A_im": inputs["L0_attn_A_im"],
        "B_re": inputs["L0_attn_B_re"], "B_im": inputs["L0_attn_B_im"],
        "C_re": inputs["L0_attn_C_re"], "C_im": inputs["L0_attn_C_im"],
        "D_re": inputs["L0_attn_D_re"], "D_im": inputs["L0_attn_D_im"],
    }


def emit_ssm(nc, block, s, cst, scst, x_in, x_out, W, scratch):
    emit_ssm_phaseU(nc, block, s, cst, x_in, W, scratch["uln"])
    emit_ssm_phaseK(nc, block, s, cst, W, scratch["kk"])
    emit_ssm_fft(nc, block, s, cst, scst, x_in, x_out, scratch["uln"], scratch["kk"],
                 scratch["g1"], scratch["sp"])


def build_ssm_launch(debug=False):
    nc = bass.Bass("TRN2", target_bir_lowering=False)
    cst = decl_consts(nc)
    scst = decl_ssm_consts(nc)
    x_in = nc.dram_tensor("x_in", [T, D], F32, kind="ExternalInput").ap()
    x_out = nc.dram_tensor("x_out", [T, D], F32, kind="ExternalOutput").ap()
    W = ssm_weight_decls(nc)
    sk = "ExternalOutput" if debug else "Internal"
    scratch = {
        "uln": nc.dram_tensor("scr_uln", [T, D], F32, kind=sk).ap(),
        "kk": nc.dram_tensor("scr_kk", [T, D], F32, kind=sk).ap(),
        "g1": nc.dram_tensor("scr_g1", [2, 2, 128, 128, D], BF16, kind="Internal").ap(),
        "sp": nc.dram_tensor("scr_sp", [2, 2, 128, 128, D], BF16, kind="Internal").ap(),
    }
    with nc.Block() as block:
        s = Sems(nc, SEM_NAMES + ["pos"])
        emit_ssm(nc, block, s, cst, scst, x_in, x_out, W, scratch)
        @block.scalar
        def _(a):
            a.wait_ge(s.h["qs"], s.c["qs"])
        s.close()
    return nc


def xl_consts():
    tri = (np.triu(np.ones((128, 128)), 1) * 1.0).astype(np.float32)
    bigi = np.zeros((128, 16 * 128), np.float32)
    for h in range(16):
        bigi[:, h * 128:(h + 1) * 128] = np.eye(128) * 1e30
    inda = np.zeros((16, 16 * 128), np.float32)
    for h in range(16):
        inda[h, h * 128:(h + 1) * 128] = 1.0
    ones16 = np.ones((16, 128), np.float32)
    return {
        "cst_triu": tri, "cst_bigi": bigi, "cst_inda": inda, "cst_ones16": ones16,
        "cst_ones128b": np.ones((128, 8), _bf16),
        "cst_ones128f": np.ones((128, 8), np.float32),
        "cst_ones64": np.ones((1, 64), np.float32),
    }


def decl_xl_consts(nc):
    return {
        "cst_triu": nc.dram_tensor("cst_triu", [128, 128], F32, kind="ExternalInput").ap(),
        "cst_bigi": nc.dram_tensor("cst_bigi", [128, 16 * 128], F32, kind="ExternalInput").ap(),
        "cst_inda": nc.dram_tensor("cst_inda", [16, 16 * 128], F32, kind="ExternalInput").ap(),
        "cst_ones16": nc.dram_tensor("cst_ones16", [16, 128], F32, kind="ExternalInput").ap(),
        "cst_ones128b": nc.dram_tensor("cst_ones128b", [128, 8], BF16, kind="ExternalInput").ap(),
        "cst_ones128f": nc.dram_tensor("cst_ones128f", [128, 8], F32, kind="ExternalInput").ap(),
        "cst_ones64": nc.dram_tensor("cst_ones64", [1, 64], F32, kind="ExternalInput").ap(),
    }


def xl_weight_decls(nc, pfx=""):
    return {
        "ln1_w": nc.dram_tensor(pfx + "ln1_w", [D], F32, kind="ExternalInput").ap(),
        "ln2_w": nc.dram_tensor(pfx + "ln2_w", [D], F32, kind="ExternalInput").ap(),
        "wq": nc.dram_tensor(pfx + "wq", [D, D], F32, kind="ExternalInput").ap(),
        "wk": nc.dram_tensor(pfx + "wk", [D, D], F32, kind="ExternalInput").ap(),
        "wv": nc.dram_tensor(pfx + "wv", [D, D], F32, kind="ExternalInput").ap(),
        "wo": nc.dram_tensor(pfx + "wo", [D, D], F32, kind="ExternalInput").ap(),
        "wi": nc.dram_tensor(pfx + "wi", [D, 16], F32, kind="ExternalInput").ap(),
        "bi": nc.dram_tensor(pfx + "bi", [16], F32, kind="ExternalInput").ap(),
        "wf": nc.dram_tensor(pfx + "wf", [D, 16], F32, kind="ExternalInput").ap(),
        "bf": nc.dram_tensor(pfx + "bf", [16], F32, kind="ExternalInput").ap(),
        "w_gate": nc.dram_tensor(pfx + "w_gate", [D, 2816], F32, kind="ExternalInput").ap(),
        "w_up": nc.dram_tensor(pfx + "w_up", [D, 2816], F32, kind="ExternalInput").ap(),
        "w_down": nc.dram_tensor(pfx + "w_down", [2816, D], F32, kind="ExternalInput").ap(),
    }


def xl_weight_inmap(inputs):
    return {k: inputs["L1_attn_" + k] for k in
            ["ln1_w", "ln2_w", "wq", "wk", "wv", "wo", "wi", "bi", "wf", "bf",
             "w_gate", "w_up", "w_down"]}


def emit_xl_phaseA(nc, block, s, cst, x_in, W, zbuf, q_hbm, k_hbm, v_hbm, gi_hbm, gf_hbm):
    with ExitStack() as _es:
        _e = _es.enter_context
        xb = _e(_sb(nc, "X_xb", [128, 2, 1024], F32))
        sq = _e(_sb(nc, "X_sq", [128, 1024], F32))
        st16 = _e(_sb(nc, "X_st", [128, 64], F32))
        zf = _e(_sb(nc, "X_zf", [128, 2, 1024], F32))
        zr = _e(_sb(nc, "X_zr", [128, 1024], F32))
        h1b = _e(_sb(nc, "X_h1b", [128, 2, 1024], BF16))
        h1fm = _e(_sb(nc, "X_h1fm", [128, 2, 8 * 512], BF16))
        eyebf = _e(_sb(nc, "X_eyebf", [128, 128], BF16))
        lnw1 = _e(_sb(nc, "X_lnw1", [128, 1024], F32))
        grow = _e(_sb(nc, "X_grow", [1, 1024], F32))
        ronep = _e(_sb(nc, "X_onep", [1, 128], F32))
        wqs = _e(_sb(nc, "X_wq", [128, 8 * 1024], BF16))
        wks = _e(_sb(nc, "X_wk", [128, 8 * 1024], BF16))
        wvs = _e(_sb(nc, "X_wv", [128, 8 * 1024], BF16))
        wifs = _e(_sb(nc, "X_wif", [128, 8 * 32], BF16))
        bif = _e(_sb(nc, "X_bif", [16, 2], F32))
        qkvo = _e(_sb(nc, "X_qkvo", [128, 3, 8 * 512], BF16))
        rawo = _e(_sb(nc, "X_rawo", [16, 2, 2, 512], F32))
        psT = _e(_ps(nc, "X_psT", [128, 1024], BF16))
        psQ = _e(_ps(nc, "X_psQ", [128, 3, 512], F32))
        psV2 = _e(_ps(nc, "X_psV2", [128, 1024], F32))
        psg = _e(_ps(nc, "X_psg", [16, 2, 512], F32))

        @block.sync
        def _(q):
            s.inc(q.dma_start(out=eyebf[:], in_=cst["cst_eye_bf"][:, :]), "qx", 16)
            s.inc(q.dma_start(out=grow[0:1, :], in_=W["ln1_w"][None, :]), "qx", 16)
            s.inc(q.dma_start(out=ronep[0:1, :], in_=cst["cst_oneh"][0:1, 0:128]), "qx", 16)
            s.inc(q.dma_start(out=bif[:, 0], in_=W["bi"][:]), "qx", 16)
            s.inc(q.dma_start(out=bif[:, 1], in_=W["bf"][:]), "qx", 16)
        @block.gpsimd
        def _(gq):
            for wsb, wname in [(wqs, "wq"), (wks, "wk"), (wvs, "wv")]:
                for k in range(DC):
                    s.inc(gq.dma_start(out=wsb[:, k * 1024:(k + 1) * 1024],
                                       in_=W[wname].rearrange("(k p) d -> k p d", p=128)[k]), "qw", 16)
            for k in range(DC):
                s.inc(gq.dma_start(out=wifs[:, k * 32:k * 32 + 16],
                                   in_=W["wi"].rearrange("(k p) d -> k p d", p=128)[k]), "qw", 16)
                s.inc(gq.dma_start(out=wifs[:, k * 32 + 16:k * 32 + 32],
                                   in_=W["wf"].rearrange("(k p) d -> k p d", p=128)[k]), "qw", 16)
        prep_qx = s.c["qx"]; prep_qw = s.c["qw"]
        @block.tensor
        def _(tn):
            tn.wait_ge(s.h["qx"], prep_qx)
            for hh in range(2):
                i = tn.matmul(psV2[:, hh * 512:(hh + 1) * 512], ronep[0:1, :],
                              grow[0:1, hh * 512:(hh + 1) * 512], start=True, stop=True)
            s.inc(i, "tn")
        lb_c = s.c["tn"]
        @block.vector
        def _(v):
            v.wait_ge(s.h["tn"], lb_c)
            s.inc(v.tensor_copy(lnw1[:], psV2[:]), "ve")
        lbv_c = s.c["ve"]

        for g in range(16):
            gslot = g % 2
            for ti in range(4):
                t = g * 4 + ti
                tslot = t % 2
                @block.sync
                def _(q):
                    q.wait_ge(s.h["ve"], s.get(f"xxb_used_{tslot}"))
                    s.inc(q.dma_start(out=xb[:, tslot, :], in_=x_in[t * 128:(t + 1) * 128, :]), "qx", 16)
                ld_c = s.c["qx"]
                pw = (lambda e: e.wait_ge(s.h["qs"], s.get(f"zf_st_{tslot}")))
                emit_ln_tile(block, s, xb[:, tslot, :], zf[:, tslot, :], sq,
                             st16[:, tslot * 16:(tslot + 1) * 16], ld_sem="qx", ld_cnt=ld_c,
                             pre_wait=pw)
                s.c[f"xxb_used_{tslot}"] = s.c["ve"]
                zf_c = s.c["ve"]
                @block.scalar
                def _(a):
                    a.wait_ge(s.h["ve"], zf_c)
                    s.inc(a.dma_start(out=zbuf[t * 128:(t + 1) * 128, :], in_=zf[:, tslot, :]), "qs", 16)
                    s.c[f"zf_st_{tslot}"] = s.c["qs"]
                emit_ln_tile(block, s, zf[:, tslot, :], zr[:], sq,
                             st16[:, 32 + tslot * 16:32 + (tslot + 1) * 16], eps=1e-6, rms_w=True)
                @block.vector
                def _(v):
                    v.wait_ge(s.h["ve"], lbv_c)
                    v.wait_ge(s.h["tn"], s.get(f"xh1_used_{tslot}"))
                    s.inc(v.tensor_tensor(h1b[:, tslot, :], zr[:], lnw1[:], OP.mult), "ve")
                h1_c = s.c["ve"]
                @block.tensor
                def _(tn):
                    tn.wait_ge(s.h["ve"], h1_c)
                    tn.wait_ge(s.h["ve"], s.get("xpsT_used"))
                    for c in range(DC):
                        i = tn.transpose(psT[:, c * 128:(c + 1) * 128],
                                         h1b[:, tslot, c * 128:(c + 1) * 128], eyebf[:])
                    s.inc(i, "tn")
                s.c[f"xh1_used_{tslot}"] = s.c["tn"]
                tr_c = s.c["tn"]
                @block.vector
                def _(v):
                    v.wait_ge(s.h["tn"], tr_c)
                    if ti == 0:
                        v.wait_ge(s.h["tn"], s.get(f"xh1fm_used_{gslot}"))
                    s.inc(v.tensor_copy(
                        h1fm[:, gslot, :].rearrange("p (c l) -> p c l", c=DC)[:, :, ti * 128:(ti + 1) * 128],
                        psT[:].rearrange("p (c l) -> p c l", c=DC)), "ve")
                    s.c["xpsT_used"] = s.c["ve"]
            h1g_c = s.c["ve"]
            qkv_ac_base = s.c["ac"]
            @block.tensor
            def _(tn):
                tn.wait_ge(s.h["ve"], h1g_c)
                tn.wait_ge(s.h["qw"], prep_qw)
                nmm = 0
                for m in range(DC):
                    for wi_, wsb in enumerate([wqs, wks, wvs]):
                        if nmm >= 3:
                            tn.wait_ge(s.h["ac"], qkv_ac_base + nmm - 2)
                        for k in range(DC):
                            i = tn.matmul(psQ[:, nmm % 3, :], wsb[:, k * 1024 + m * 128: k * 1024 + m * 128 + 128],
                                          h1fm[:, gslot, k * 512:(k + 1) * 512],
                                          start=(k == 0), stop=(k == DC - 1))
                        s.inc(i, "tn")
                        nmm += 1
            qkv_base = s.c["tn"] - 24
            @block.scalar
            def _(a):
                if g > 0:
                    a.wait_ge(s.h["qs"], s.get("qkv_st"))
                nmm = 0
                for m in range(DC):
                    for wi_ in range(3):
                        a.wait_ge(s.h["tn"], qkv_base + nmm + 1)
                        s.inc(a.copy(qkvo[:, wi_, m * 512:(m + 1) * 512], psQ[:, nmm % 3, :]), "ac")
                        nmm += 1
            qkv_done = s.c["ac"]
            @block.scalar
            def _(a):
                a.wait_ge(s.h["ac"], qkv_done)
                for wi_, hbm in enumerate([q_hbm, k_hbm, v_hbm]):
                    s.inc(a.dma_start(out=hbm.rearrange("(c p) l -> p c l", p=128)[:, :, g * 512:(g + 1) * 512],
                                      in_=qkvo[:, wi_, :].rearrange("p (c l) -> p c l", c=DC)), "qs", 16)
                s.c["qkv_st"] = s.c["qs"]
            @block.tensor
            def _(tn):
                tn.wait_ge(s.h["ve"], h1g_c)
                if g > 1:
                    tn.wait_ge(s.h["ac"], s.get(f"gate_evac_{gslot}"))
                for gg in range(2):
                    for k in range(DC):
                        i = tn.matmul(psg[:, gg, :], wifs[:, k * 32 + gg * 16: k * 32 + gg * 16 + 16],
                                      h1fm[:, gslot, k * 512:(k + 1) * 512],
                                      start=(k == 0), stop=(k == DC - 1))
                    s.inc(i, "tn")
            g_c = s.c["tn"]
            @block.scalar
            def _(a):
                a.wait_ge(s.h["tn"], g_c)
                if g > 1:
                    a.wait_ge(s.h["qs"], s.get(f"gate_st_{gslot}"))
                s.inc(a.activation(rawo[:, gslot, 0, :], psg[:, 0, :], AF.Identity, bias=bif[:, 0:1]), "ac")
                s.inc(a.activation(rawo[:, gslot, 1, :], psg[:, 1, :], AF.Identity, bias=bif[:, 1:2]), "ac")
                s.c[f"gate_evac_{gslot}"] = s.c["ac"]
            ge_c = s.c["ac"]
            @block.scalar
            def _(a):
                a.wait_ge(s.h["ac"], ge_c)
                s.inc(a.dma_start(out=gi_hbm[:, g * 512:(g + 1) * 512], in_=rawo[:, gslot, 0, :]), "qs", 16)
                s.inc(a.dma_start(out=gf_hbm[:, g * 512:(g + 1) * 512], in_=rawo[:, gslot, 1, :]), "qs", 16)
                s.c[f"gate_st_{gslot}"] = s.c["qs"]
    phase_barrier(block, s)


def emit_xl_phaseAb(nc, block, s, gi_hbm, gf_hbm, la, ci):
    with ExitStack() as _es:
        _e = _es.enter_context
        lsq = _e(_sb(nc, "Xb_lsq", [16, 8192], F32))
        @block.sync
        def _(q):
            s.inc(q.dma_start(out=ci[:, :], in_=gi_hbm[:, :]), "qx", 16)
            s.inc(q.dma_start(out=la[:, :], in_=gf_hbm[:, :]), "qx", 16)
        l_c = s.c["qx"]
        @block.vector
        def _(v):
            v.wait_ge(s.h["qx"], l_c)
            s.inc(v.tensor_scalar(ci[:, :], ci[:, :], -10.0, 10.0, OP.max, OP.min), "ve")
        ci_c = s.c["ve"]
        @block.scalar
        def _(a):
            a.wait_ge(s.h["qx"], l_c)
            s.inc(a.activation(lsq[:, :], la[:, :], AF.Abs), "ac")
            s.inc(a.activation(lsq[:, :], lsq[:, :], AF.Exp, scale=-1.0), "ac")
            s.inc(a.activation(lsq[:, :], lsq[:, :], AF.Ln, bias=1.0), "ac")
        ls_c = s.c["ac"]
        @block.vector
        def _(v):
            v.wait_ge(s.h["ac"], ls_c)
            s.edge(v, v.tensor_scalar_min(la[:, :], la[:, :], 0.0))
            s.edge(v, v.tensor_tensor(la[:, :], la[:, :], lsq[:, :], OP.subtract))
            bufs = [la[:, :], lsq[:, :]]
            cur = 0
            step = 1
            while step < 128:
                c3 = bufs[cur].rearrange("p (q c) -> p q c", c=128)
                o3 = bufs[1 - cur].rearrange("p (q c) -> p q c", c=128)
                s.edge(v, v.tensor_copy(o3[:, :, 0:step], c3[:, :, 0:step]))
                s.edge(v, v.tensor_tensor(o3[:, :, step:128], c3[:, :, step:128],
                                          c3[:, :, 0:128 - step], OP.add))
                cur = 1 - cur
                step *= 2
            if cur == 1:
                s.edge(v, v.tensor_copy(la[:, :], lsq[:, :]))
            s.inc(v.tensor_copy(lsq[:, 0:16], la[:, 0:16]), "ve")
    phase_barrier(block, s)


def emit_xl_phaseB(nc, block, s, cst, xcst, q_hbm, k_hbm, v_hbm, la, ci, hb_hbm, dbg=None):
    NCH = 64
    with ExitStack() as _es:
        _e = _es.enter_context
        qhf = _e(_sb(nc, "B_q", [64, 2, 16 * 128], BF16))
        khf = _e(_sb(nc, "B_k", [64, 2, 16 * 128], BF16))
        vhf = _e(_sb(nc, "B_v", [64, 2, 16 * 128], BF16))
        triu = _e(_sb(nc, "B_triu", [128, 128], F32))
        bigi = _e(_sb(nc, "B_bigi", [128, 16 * 128], F32))
        inda = _e(_sb(nc, "B_inda", [16, 16 * 128], F32))
        ones16 = _e(_sb(nc, "B_ones16", [16, 128], F32))
        ones16b = _e(_sb(nc, "B_ones16b", [16, 128], BF16))
        ones128b = _e(_sb(nc, "B_ones128b", [128, 8], BF16))
        ones128f = _e(_sb(nc, "B_ones128f", [128, 8], F32))
        ones64 = _e(_sb(nc, "B_ones64", [1, 64], F32))
        eyef = _e(_sb(nc, "B_eyef", [128, 128], F32))
        eyebf = _e(_sb(nc, "B_eyebf", [128, 128], BF16))
        cst_ = _e(_sb(nc, "B_Cst", [64, 1024], F32))
        cbf = _e(_sb(nc, "B_Cbf", [64, 1024], BF16))
        nst = _e(_sb(nc, "B_nst", [64, 16], F32))
        nbf = _e(_sb(nc, "B_nbf", [64, 16], BF16))
        nsc = _e(_sb(nc, "B_nsc", [64, 64], F32))
        gp = _e(_sb(nc, "B_gp", [16, 5 * 128], F32))
        biasT = _e(_sb(nc, "B_biasT", [128, 16], F32))
        gsT = _e(_sb(nc, "B_gsT", [128, 16], F32))
        decT = _e(_sb(nc, "B_decT", [1, 16], F32))
        emask = _e(_sb(nc, "B_emask", [16, 2048], BF16))
        nlam = _e(_sb(nc, "B_nlam", [16, 2048], F32))
        explaB = _e(_sb(nc, "B_explaB", [128, 2048], BF16))
        dsb = _e(_sb(nc, "B_D", [128, 2048], F32))
        psb = _e(_sb(nc, "B_P", [128, 2048], BF16))
        psb32 = _e(_sb(nc, "B_P32", [128, 2048], F32))
        qsc32 = _e(_sb(nc, "B_qsc32", [64, 2048], F32))
        qsc = _e(_sb(nc, "B_qsc", [64, 2048], BF16))
        ktm = _e(_sb(nc, "B_ktm", [128, 1024], BF16))
        vtm = _e(_sb(nc, "B_vtm", [128, 1024], BF16))
        kdec = _e(_sb(nc, "B_kdec", [128, 1024], BF16))
        dnsb = _e(_sb(nc, "B_dn", [1, 2048], F32))
        recsb = _e(_sb(nc, "B_rec", [1, 2048], F32))
        recB = _e(_sb(nc, "B_recB", [64, 2048], F32))
        hfin = _e(_sb(nc, "B_hfin", [64, 2048], BF16))
        tsc = _e(_sb(nc, "B_tsc", [64, 1024], F32))
        psA = _e(_ps(nc, "B_psA", [128, 4, 512], F32))
        psB = _e(_ps(nc, "B_psB", [128, 4, 512], F32))

        @block.sync
        def _(q):
            s.inc(q.dma_start(out=triu[:], in_=xcst["cst_triu"][:, :]), "qx", 16)
            s.inc(q.dma_start(out=bigi[:], in_=xcst["cst_bigi"][:, :]), "qx", 16)
            s.inc(q.dma_start(out=inda[:], in_=xcst["cst_inda"][:, :]), "qx", 16)
            s.inc(q.dma_start(out=ones16[:], in_=xcst["cst_ones16"][:, :]), "qx", 16)
            s.inc(q.dma_start(out=ones128b[:], in_=xcst["cst_ones128b"][:, :]), "qx", 16)
            s.inc(q.dma_start(out=ones128f[:], in_=xcst["cst_ones128f"][:, :]), "qx", 16)
            s.inc(q.dma_start(out=ones64[0:1, :], in_=xcst["cst_ones64"][:, :]), "qx", 16)
            s.inc(q.dma_start(out=eyef[:], in_=cst["cst_eye_f32"][:, :]), "qx", 16)
            s.inc(q.dma_start(out=eyebf[:], in_=cst["cst_eye_bf"][:, :]), "qx", 16)
        bp_c = s.c["qx"]
        @block.vector
        def _(v):
            v.wait_ge(s.h["qx"], bp_c)
            s.edge(v, v.memset(cst_[:], 0.0))
            s.edge(v, v.memset(cbf[:], 0.0))
            s.edge(v, v.memset(nst[:], 0.0))
            s.edge(v, v.memset(nbf[:], 0.0))
            s.inc(v.tensor_copy(ones16b[:], ones16[:]), "ve")
        init_c = s.c["ve"]

        for qc in range(NCH):
            slot = qc % 2
            la_c = la[:, qc * 128:(qc + 1) * 128]
            ci_c = ci[:, qc * 128:(qc + 1) * 128]
            @block.sync
            def _(q):
                q.wait_ge(s.h["tn"], s.get(f"bqkv_used_{slot}"))
                q.wait_ge(s.h["ve"], s.get(f"bq_used_ve_{slot}"))
                for sb_, hbm in [(qhf, q_hbm), (khf, k_hbm), (vhf, v_hbm)]:
                    s.inc(q.dma_start(out=sb_[:, slot, :].rearrange("p (h c) -> p h c", h=16),
                                      in_=hbm[:, qc * 128:(qc + 1) * 128].rearrange(
                                          "(h ee) c -> ee h c", ee=64)), "qx", 16)
            ld_c = s.c["qx"]
            @block.vector
            def _(v):
                v.wait_ge(s.h["ve"], init_c)
                v.wait_ge(s.h["tn"], s.get("b_gp_used_tn"))
                s.edge(v, v.tensor_scalar_mul(gp[:, 0:128], la_c, -1.0))
                s.edge(v, v.tensor_scalar_add(gp[:, 128:256], gp[:, 0:128], la_c[:, 127:128]))
                s.inc(v.tensor_tensor(gp[:, 128:256], gp[:, 128:256], ci_c, OP.add), "ve")
            gp_c = s.c["ve"]
            @block.scalar
            def _(a):
                a.wait_ge(s.h["ve"], gp_c)
                s.edge(a, a.activation(gp[:, 256:384], gp[:, 128:256], AF.Exp), "as_")
                s.edge(a, a.activation(gp[:, 384:512], la_c, AF.Exp), "as_")
                s.inc(a.activation(gp[:, 128:129], la_c[:, 127:128], AF.Exp), "ac")
            act1_c = s.c["ac"]
            @block.vector
            def _(v):
                s.inc(v.tensor_tensor(gp[:, 512:640], ci_c, la_c, OP.subtract), "ve")
            lc_c = s.c["ve"]
            @block.tensor
            def _(tn):
                tn.wait_ge(s.h["ve"], lc_c)
                tn.wait_ge(s.h["ac"], act1_c)
                tn.wait_ge(s.h["ve"], s.get("b_ps3_used"))
                tn.transpose(psB[:, 3, 0:16], gp[:, 512:640], eyef[0:16, 0:16])
                tn.transpose(psB[:, 3, 16:32], gp[:, 256:384], eyef[0:16, 0:16])
                s.inc(tn.transpose(psB[0:1, 3, 32:48], gp[:, 128:129], eyef[0:16, 0:16]), "tn")
            t1_c = s.c["tn"]
            @block.vector
            def _(v):
                v.wait_ge(s.h["tn"], t1_c)
                v.wait_ge(s.h["ac"], s.get("b_biasT_used"))
                v.wait_ge(s.h["tn"], s.get("b_decT_used"))
                s.edge(v, v.tensor_copy(biasT[:], psB[:, 3, 0:16]))
                s.edge(v, v.tensor_copy(gsT[:], psB[:, 3, 16:32]))
                s.inc(v.tensor_copy(decT[0:1, :], psB[0:1, 3, 32:48]), "ve")
                s.c["b_ps3_used"] = s.c["ve"]
            t1v_c = s.c["ve"]
            @block.vector
            def _(v):
                v.wait_ge(s.h["ac"], act1_c)
                v.wait_ge(s.h["tn"], s.get("b_emask_used"))
                s.edge(v, v.tensor_tensor(emask[:].rearrange("p (h c) -> p h c", h=16),
                                          gp[:, 384:512][:, None, :].broadcast_to([16, 16, 128]),
                                          inda[:].rearrange("p (h c) -> p h c", h=16), OP.mult))
                s.inc(v.tensor_tensor(nlam[:].rearrange("p (h c) -> p h c", h=16),
                                      gp[:, 0:128][:, None, :].broadcast_to([16, 16, 128]),
                                      inda[:].rearrange("p (h c) -> p h c", h=16), OP.mult), "ve")
            em_c = s.c["ve"]
            @block.tensor
            def _(tn):
                tn.wait_ge(s.h["ve"], em_c)
                tn.wait_ge(s.h["ve"], s.get("b_psB_free"))
                tn.wait_ge(s.h["ve"], t1v_c)
                for b in range(4):
                    i = tn.matmul(psB[:, b, :], ones16b[:], emask[:, b * 512:(b + 1) * 512],
                                  start=True, stop=True)
                s.inc(i, "tn")
                s.c["b_emask_used"] = s.c["tn"]
            eb_c = s.c["tn"]
            @block.vector
            def _(v):
                v.wait_ge(s.h["tn"], eb_c)
                s.inc(v.tensor_copy(explaB[:], psB[:].rearrange("p b l -> p (b l)")), "ve")
                s.c["b_psB_free"] = s.c["ve"]
                s.c["b_ps3_used"] = s.c["ve"]
            ebv_c = s.c["ve"]
            @block.tensor
            def _(tn):
                tn.wait_ge(s.h["ve"], em_c)
                tn.wait_ge(s.h["ve"], s.get("b_psA_last_ve"))
                for b in range(4):
                    tn.matmul(psA[:, b, :], ones16[:], nlam[:, b * 512:(b + 1) * 512],
                              start=True, stop=False)
                for b in range(4):
                    i = tn.matmul(psA[:, b, :], triu[:], bigi[:, b * 512:(b + 1) * 512],
                                  start=False, stop=True)
                s.inc(i, "tn")
                s.c["b_gp_used_tn"] = s.c["tn"]
                s.c["b_emask_used"] = s.c["tn"]
            n_c = s.c["tn"]
            @block.scalar
            def _(a):
                a.wait_ge(s.h["tn"], n_c)
                a.wait_ge(s.h["ve"], t1v_c)
                for h in range(16):
                    i = a.activation(dsb[:, h * 128:(h + 1) * 128],
                                     psA[:, h // 4, (h % 4) * 128:(h % 4 + 1) * 128],
                                     AF.Exp, scale=-1.0, bias=biasT[:, h:h + 1])
                    s.inc(i, "ac")
                s.c["b_biasT_used"] = s.c["ac"]
            d_c = s.c["ac"]
            @block.tensor
            def _(tn):
                tn.wait_ge(s.h["ac"], d_c)
                tn.wait_ge(s.h["qx"], ld_c)
                for h in range(16):
                    i = tn.matmul(psA[0:128, h // 4, (h % 4) * 128:(h % 4 + 1) * 128],
                                  khf[:, slot, h * 128:(h + 1) * 128],
                                  qhf[:, slot, h * 128:(h + 1) * 128], start=True, stop=True)
                s.inc(i, "tn")
            s_c = s.c["tn"]
            if dbg is not None and qc == 18:
                @block.scalar
                def _(a):
                    a.wait_ge(s.h["ac"], d_c)
                    s.inc(a.dma_start(out=dbg["dsb"][:, :], in_=dsb[:]), "qs", 16)
            @block.vector
            def _(v):
                v.wait_ge(s.h["tn"], s_c)
                s.edge(v, v.tensor_tensor(psb32[:], psA[:].rearrange("p b l -> p (b l)"), dsb[:], OP.mult))
                s.edge(v, v.tensor_copy(psb[:], psb32[:]))
                v.wait_ge(s.h["ve"], ebv_c)
                s.edge(v, v.tensor_tensor(qsc32[:], qhf[0:64, slot, :], explaB[0:64, :], OP.mult))
                s.inc(v.tensor_copy(qsc[:], qsc32[:]), "ve")
                s.c[f"bq_used_ve_{slot}"] = s.c["ve"]
            p_c = s.c["ve"]
            if dbg is not None and qc == 18:
                @block.scalar
                def _(a):
                    a.wait_ge(s.h["ve"], p_c)
                    s.inc(a.dma_start(out=dbg["psb"][:, :], in_=psb[:]), "qs", 16)
                    s.inc(a.dma_start(out=dbg["qsc"][:, :], in_=qsc[:]), "qs", 16)
                    s.inc(a.dma_start(out=dbg["explaB"][:, :], in_=explaB[:]), "qs", 16)
            @block.tensor
            def _(tn):
                tn.wait_ge(s.h["ve"], s.get("b_psB_free"))
                tn.wait_ge(s.h["qx"], ld_c)
                for h in range(16):
                    tn.transpose(psB[:, 0, :].bitcast(BF16)[:, h * 64:(h + 1) * 64],
                                 khf[:, slot, h * 128:(h + 1) * 128], eyebf[0:64, 0:64])
                    i = tn.transpose(psB[:, 1, :].bitcast(BF16)[:, h * 64:(h + 1) * 64],
                                     vhf[:, slot, h * 128:(h + 1) * 128], eyebf[0:64, 0:64])
                s.inc(i, "tn")
                s.c[f"bqkv_used_{slot}"] = s.c["tn"]
            kv_c = s.c["tn"]
            @block.vector
            def _(v):
                v.wait_ge(s.h["tn"], kv_c)
                s.edge(v, v.tensor_copy(ktm[:], psB[:, 0, :].bitcast(BF16)))
                s.inc(v.tensor_copy(vtm[:], psB[:, 1, :].bitcast(BF16)), "ve")
                s.c["b_psB_free"] = s.c["ve"]
            kv_v = s.c["ve"]
            @block.vector
            def _(v):
                s.inc(v.tensor_tensor(kdec[:].rearrange("p (h ee) -> p h ee", h=16), ktm[:].rearrange("p (h ee) -> p h ee", h=16),
                                      gsT[:, :, None].broadcast_to([128, 16, 64]), OP.mult), "ve")
            kd_c = s.c["ve"]
            @block.tensor
            def _(tn):
                tn.wait_ge(s.h["ve"], p_c)
                tn.wait_ge(s.h["ve"], kv_v)
                for h in range(16):
                    tn.matmul(psA[0:64, h // 4, (h % 4) * 128:(h % 4 + 1) * 128],
                              vtm[:, h * 64:(h + 1) * 64], psb[:, h * 128:(h + 1) * 128],
                              start=True, stop=False)
                    i = tn.matmul(psA[0:64, h // 4, (h % 4) * 128:(h % 4 + 1) * 128],
                                  cbf[:, h * 64:(h + 1) * 64], qsc[:, h * 128:(h + 1) * 128],
                                  start=False, stop=True)
                s.inc(i, "tn")
            ii_c = s.c["tn"]
            @block.tensor
            def _(tn):
                tn.wait_ge(s.h["ve"], s.get("b_psB_free"))
                for b in range(4):
                    tn.matmul(psB[0:1, b, :], ones128f[:, 0:1], psb32[:, b * 512:(b + 1) * 512],
                              start=True, stop=False)
                    for hh in range(4):
                        h = b * 4 + hh
                        i = tn.matmul(psB[0:1, b, hh * 128:(hh + 1) * 128],
                                      nst[:, h:h + 1], qsc32[:, h * 128:(h + 1) * 128],
                                      start=False, stop=(hh == 3))
                s.inc(i, "tn")
            dn_c = s.c["tn"]
            @block.scalar
            def _(a):
                a.wait_ge(s.h["tn"], dn_c)
                s.inc(a.activation(dnsb[0:1, :], psB[0:1, :, :].rearrange("p b l -> p (b l)"), AF.Abs), "ac")
            ab_c = s.c["ac"]
            @block.vector
            def _(v):
                v.wait_ge(s.h["ac"], ab_c)
                s.edge(v, v.tensor_scalar_max(dnsb[0:1, :], dnsb[0:1, :], 1.0))
                s.inc(v.reciprocal(recsb[0:1, :], dnsb[0:1, :]), "ve")
            rc_c = s.c["ve"]
            @block.tensor
            def _(tn):
                tn.wait_ge(s.h["ve"], rc_c)
                for b in range(4):
                    i = tn.matmul(psB[0:64, b, :], ones64[0:1, :], recsb[0:1, b * 512:(b + 1) * 512],
                                  start=True, stop=True)
                s.inc(i, "tn")
            rb_c = s.c["tn"]
            @block.vector
            def _(v):
                v.wait_ge(s.h["tn"], rb_c)
                s.inc(v.tensor_copy(recB[:], psB[0:64, :, :].rearrange("p b l -> p (b l)")), "ve")
                s.c["b_psB_free"] = s.c["ve"]
            rbv_c = s.c["ve"]
            if dbg is not None and qc == 18:
                @block.scalar
                def _(a):
                    a.wait_ge(s.h["ve"], rbv_c)
                    s.inc(a.dma_start(out=dbg["recB"][:, :], in_=recB[:]), "qs", 16)
                    s.inc(a.dma_start(out=dbg["vtm"][:, :], in_=vtm[:]), "qs", 16)
            @block.vector
            def _(v):
                v.wait_ge(s.h["tn"], ii_c)
                v.wait_ge(s.h["qs"], s.get("b_h_st"))
                s.inc(v.tensor_tensor(hfin[:], psA[0:64, :, :].rearrange("p b l -> p (b l)"),
                                      recB[:], OP.mult), "ve")
            hf_c = s.c["ve"]
            @block.scalar
            def _(a):
                a.wait_ge(s.h["ve"], hf_c)
                s.inc(a.dma_start(out=hb_hbm[:, qc * 128:(qc + 1) * 128].rearrange("(h dd) c -> dd h c", dd=64),
                                  in_=hfin[:].rearrange("p (h c) -> p h c", h=16)), "qs", 16)
                s.c["b_h_st"] = s.c["qs"]
            @block.tensor
            def _(tn):
                tn.wait_ge(s.h["ve"], hf_c)
                tn.wait_ge(s.h["ve"], kd_c)
                for h in range(16):
                    tn.matmul(psA[0:64, h // 8, (h % 8) * 64:(h % 8 + 1) * 64],
                              kdec[:, h * 64:(h + 1) * 64], vtm[:, h * 64:(h + 1) * 64],
                              start=True, stop=True)
                i = tn.matmul(psA[0:64, 2, 0:16], ones64[0:1, :], decT[0:1, :],
                              start=True, stop=True)
                s.inc(i, "tn")
                s.c["b_decT_used"] = s.c["tn"]
            ct_c = s.c["tn"]
            @block.vector
            def _(v):
                v.wait_ge(s.h["tn"], ct_c)
                s.edge(v, v.tensor_copy(nsc[:, 0:16], psA[0:64, 2, 0:16]))
                s.edge(v, v.tensor_tensor(tsc[:].rearrange("p (h ee) -> p h ee", h=16),
                                          cst_[:].rearrange("p (h ee) -> p h ee", h=16),
                                          nsc[:, 0:16][:, :, None].broadcast_to([64, 16, 64]), OP.mult))
                s.edge(v, v.tensor_tensor(cst_[:], tsc[:], psA[0:64, 0:2, :].rearrange("p b l -> p (b l)"), OP.add))
                s.inc(v.tensor_copy(cbf[:], cst_[:]), "ve")
            cu_c = s.c["ve"]
            @block.tensor
            def _(tn):
                tn.wait_ge(s.h["ve"], s.get("b_psB_free"))
                tn.wait_ge(s.h["ve"], kd_c)
                for h in range(16):
                    i = tn.matmul(psB[0:64, 0, h:h + 1], kdec[:, h * 64:(h + 1) * 64],
                                  ones128b[:, 0:1], start=True, stop=True)
                s.inc(i, "tn")
            nt_c = s.c["tn"]
            if dbg is not None and qc in (0, 17):
                nm = "cst0" if qc == 0 else "cst17"
                @block.scalar
                def _(a):
                    a.wait_ge(s.h["ve"], cu_c)
                    s.inc(a.dma_start(out=dbg[nm][:, :], in_=cst_[:]), "qs", 16)
                    s.inc(a.dma_start(out=dbg["kdec0"][:, :], in_=kdec[:]), "qs", 16)
                    s.inc(a.dma_start(out=dbg["gsT0"][:, :], in_=gsT[:]), "qs", 16)
            @block.vector
            def _(v):
                v.wait_ge(s.h["tn"], nt_c)
                s.edge(v, v.tensor_copy(nsc[:, 16:32], psB[0:64, 0, 0:16]))
                s.edge(v, v.tensor_tensor(nsc[:, 32:48], nst[:], nsc[:, 0:16], OP.mult))
                s.edge(v, v.tensor_tensor(nst[:], nsc[:, 32:48], nsc[:, 16:32], OP.add))
                s.inc(v.tensor_copy(nbf[:], nst[:]), "ve")
                s.c["b_psB_free"] = s.c["ve"]
                s.c["b_psA_last_ve"] = s.c["ve"]
    phase_barrier(block, s)


def emit_fm_matmul_pass(nc, block, s, uid, in_hbm, w_hbm, acc_hbm, KD=D):
    KC = KD // 128
    with ExitStack() as _es:
        _e = _es.enter_context
        ws = _e(_sb(nc, uid + "_w", [128, KC * 1024], BF16))
        xs = _e(_sb(nc, uid + "_x", [128, 2, KC * 512], BF16))
        tmp = _e(_sb(nc, uid + "_t", [128, 2, 8 * 512], F32))
        psU = _e(_ps(nc, uid + "_ps", [128, 3, 512], F32))
        @block.gpsimd
        def _(gq):
            for k in range(KC):
                s.inc(gq.dma_start(out=ws[:, k * 1024:(k + 1) * 1024],
                                   in_=w_hbm.rearrange("(k p) d -> k p d", p=128)[k]), "qw", 16)
        wl_c = s.c["qw"]
        for sb in range(16):
            sslot = sb % 2
            @block.sync
            def _(q):
                q.wait_ge(s.h["tn"], s.get(uid + f"_in_used_{sslot}"))
                s.inc(q.dma_start(out=xs[:, sslot, :],
                                  in_=in_hbm.rearrange("(c p) l -> p c l", p=128)[:, :, sb * 512:(sb + 1) * 512]), "qx", 16)
            xl_c = s.c["qx"]
            ac_base = s.c["ac"]
            @block.tensor
            def _(tn):
                tn.wait_ge(s.h["qw"], wl_c)
                tn.wait_ge(s.h["qx"], xl_c)
                for m in range(DC):
                    if m >= 3:
                        tn.wait_ge(s.h["ac"], ac_base + m - 2)
                    for k in range(KC):
                        i = tn.matmul(psU[:, m % 3, :], ws[:, k * 1024 + m * 128: k * 1024 + m * 128 + 128],
                                      xs[:, sslot, k * 512:(k + 1) * 512],
                                      start=(k == 0), stop=(k == KC - 1))
                    s.inc(i, "tn")
                s.c[uid + f"_in_used_{sslot}"] = s.c["tn"]
            mm_base = s.c["tn"] - DC
            @block.scalar
            def _(a):
                if sb > 1:
                    a.wait_ge(s.h["qw"], s.get(uid + f"_t_st_{sslot}"))
                for m in range(DC):
                    a.wait_ge(s.h["tn"], mm_base + m + 1)
                    s.inc(a.copy(tmp[:, sslot, m * 512:(m + 1) * 512], psU[:, m % 3, :]), "ac")
            ev_c = s.c["ac"]
            @block.gpsimd
            def _(gq):
                gq.wait_ge(s.h["ac"], ev_c)
                s.inc(gq.dma_start(
                    out=acc_hbm.rearrange("(c p) l -> p c l", p=128)[:, :, sb * 512:(sb + 1) * 512],
                    in_=tmp[:, sslot, :].rearrange("p (c l) -> p c l", c=DC)), "qw", 16)
                s.c[uid + f"_t_st_{sslot}"] = s.c["qw"]
    phase_barrier(block, s)


def emit_rms_transpose_pass(nc, block, s, cst, uid, x_in, w_vec, out_fm, eps=1e-6):
    with ExitStack() as _es:
        _e = _es.enter_context
        xb = _e(_sb(nc, uid + "_xb", [128, 2, 1024], F32))
        sq = _e(_sb(nc, uid + "_sq", [128, 1024], F32))
        st16 = _e(_sb(nc, uid + "_st", [128, 32], F32))
        zr = _e(_sb(nc, uid + "_zr", [128, 1024], F32))
        hb = _e(_sb(nc, uid + "_hb", [128, 2, 1024], BF16))
        hfm = _e(_sb(nc, uid + "_hfm", [128, 2, 8 * 512], BF16))
        eyebf = _e(_sb(nc, uid + "_eye", [128, 128], BF16))
        lnw = _e(_sb(nc, uid + "_lnw", [128, 1024], F32))
        grow = _e(_sb(nc, uid + "_gr", [1, 1024], F32))
        ronep = _e(_sb(nc, uid + "_o1", [1, 128], F32))
        psT = _e(_ps(nc, uid + "_psT", [128, 1024], BF16))
        psW = _e(_ps(nc, uid + "_psW", [128, 1024], F32))
        @block.sync
        def _(q):
            s.inc(q.dma_start(out=eyebf[:], in_=cst["cst_eye_bf"][:, :]), "qx", 16)
            s.inc(q.dma_start(out=grow[0:1, :], in_=w_vec[None, :]), "qx", 16)
            s.inc(q.dma_start(out=ronep[0:1, :], in_=cst["cst_oneh"][0:1, 0:128]), "qx", 16)
        p_c = s.c["qx"]
        @block.tensor
        def _(tn):
            tn.wait_ge(s.h["qx"], p_c)
            for hh in range(2):
                i = tn.matmul(psW[:, hh * 512:(hh + 1) * 512], ronep[0:1, :],
                              grow[0:1, hh * 512:(hh + 1) * 512], start=True, stop=True)
            s.inc(i, "tn")
        b_c = s.c["tn"]
        @block.vector
        def _(v):
            v.wait_ge(s.h["tn"], b_c)
            s.inc(v.tensor_copy(lnw[:], psW[:]), "ve")
        bv_c = s.c["ve"]
        for g in range(16):
            gslot = g % 2
            for ti in range(4):
                t = g * 4 + ti
                tslot = t % 2
                @block.sync
                def _(q):
                    q.wait_ge(s.h["ve"], s.get(uid + f"_xb_used_{tslot}"))
                    s.inc(q.dma_start(out=xb[:, tslot, :], in_=x_in[t * 128:(t + 1) * 128, :]), "qx", 16)
                ld_c = s.c["qx"]
                emit_ln_tile(block, s, xb[:, tslot, :], zr[:], sq,
                             st16[:, tslot * 16:(tslot + 1) * 16], eps=eps, ld_sem="qx", ld_cnt=ld_c,
                             rms_w=True)
                s.c[uid + f"_xb_used_{tslot}"] = s.c["ve"]
                @block.vector
                def _(v):
                    v.wait_ge(s.h["ve"], bv_c)
                    v.wait_ge(s.h["tn"], s.get(uid + f"_hb_used_{tslot}"))
                    s.inc(v.tensor_tensor(hb[:, tslot, :], zr[:], lnw[:], OP.mult), "ve")
                h_c = s.c["ve"]
                @block.tensor
                def _(tn):
                    tn.wait_ge(s.h["ve"], h_c)
                    tn.wait_ge(s.h["ve"], s.get(uid + "_psT_used"))
                    for c in range(DC):
                        i = tn.transpose(psT[:, c * 128:(c + 1) * 128],
                                         hb[:, tslot, c * 128:(c + 1) * 128], eyebf[:])
                    s.inc(i, "tn")
                s.c[uid + f"_hb_used_{tslot}"] = s.c["tn"]
                tr_c = s.c["tn"]
                @block.vector
                def _(v):
                    v.wait_ge(s.h["tn"], tr_c)
                    if ti == 0:
                        v.wait_ge(s.h["qs"], s.get(uid + f"_fm_st_{gslot}"))
                    s.inc(v.tensor_copy(
                        hfm[:, gslot, :].rearrange("p (c l) -> p c l", c=DC)[:, :, ti * 128:(ti + 1) * 128],
                        psT[:].rearrange("p (c l) -> p c l", c=DC)), "ve")
                    s.c[uid + "_psT_used"] = s.c["ve"]
            g_c = s.c["ve"]
            @block.scalar
            def _(a):
                a.wait_ge(s.h["ve"], g_c)
                s.inc(a.dma_start(out=out_fm.rearrange("(c p) l -> p c l", p=128)[:, :, g * 512:(g + 1) * 512],
                                  in_=hfm[:, gslot, :].rearrange("p (c l) -> p c l", c=DC)), "qs", 16)
                s.c[uid + f"_fm_st_{gslot}"] = s.c["qs"]
    phase_barrier(block, s)


def emit_mlp_pass(nc, block, s, uid, h2_hbm, W, acc_hbm):
    I_ = 2816
    MI = I_ // 128
    with ExitStack() as _es:
        _e = _es.enter_context
        wg = _e(_sb(nc, uid + "_wg", [128, 8 * I_], BF16))
        wu = _e(_sb(nc, uid + "_wu", [128, 8 * I_], BF16))
        wd = _e(_sb(nc, uid + "_wd", [128, MI * 1024], BF16))
        xs = _e(_sb(nc, uid + "_x", [128, 2, 8 * 512], BF16))
        sg = _e(_sb(nc, uid + "_sg", [128, 512], BF16))
        h2m = _e(_sb(nc, uid + "_h2", [128, MI * 512], BF16))
        tmp = _e(_sb(nc, uid + "_t", [128, 8 * 512], F32))
        psG = _e(_ps(nc, uid + "_psG", [128, 3, 512], F32))
        psU = _e(_ps(nc, uid + "_psU", [128, 3, 512], F32))
        psD = _e(_ps(nc, uid + "_psD", [128, 2, 512], F32))
        @block.gpsimd
        def _(gq):
            for k in range(DC):
                s.inc(gq.dma_start(out=wg[:, k * I_:(k + 1) * I_],
                                   in_=W["w_gate"].rearrange("(k p) d -> k p d", p=128)[k]), "qw", 16)
                s.inc(gq.dma_start(out=wu[:, k * I_:(k + 1) * I_],
                                   in_=W["w_up"].rearrange("(k p) d -> k p d", p=128)[k]), "qw", 16)
            for k2 in range(MI):
                s.inc(gq.dma_start(out=wd[:, k2 * 1024:(k2 + 1) * 1024],
                                   in_=W["w_down"].rearrange("(k p) d -> k p d", p=128)[k2]), "qw", 16)
        wl_c = s.c["qw"]
        for sb in range(16):
            sslot = sb % 2
            @block.sync
            def _(q):
                q.wait_ge(s.h["tn"], s.get(uid + f"_in_used_{sslot}"))
                s.inc(q.dma_start(out=xs[:, sslot, :],
                                  in_=h2_hbm.rearrange("(c p) l -> p c l", p=128)[:, :, sb * 512:(sb + 1) * 512]), "qx", 16)
            xl_c = s.c["qx"]
            ve_base = s.c["ve"]
            @block.tensor
            def _(tn):
                tn.wait_ge(s.h["qw"], wl_c)
                tn.wait_ge(s.h["qx"], xl_c)
                for m in range(MI):
                    if m >= 3:
                        tn.wait_ge(s.h["ve"], ve_base + m - 2)
                    for k in range(DC):
                        tn.matmul(psG[:, m % 3, :], wg[:, k * I_ + m * 128: k * I_ + m * 128 + 128],
                                  xs[:, sslot, k * 512:(k + 1) * 512],
                                  start=(k == 0), stop=(k == DC - 1))
                    for k in range(DC):
                        i = tn.matmul(psU[:, m % 3, :], wu[:, k * I_ + m * 128: k * I_ + m * 128 + 128],
                                      xs[:, sslot, k * 512:(k + 1) * 512],
                                      start=(k == 0), stop=(k == DC - 1))
                    s.inc(i, "tn")
                s.c[uid + f"_in_used_{sslot}"] = s.c["tn"]
            gu_base = s.c["tn"] - MI
            for m in range(MI):
                @block.scalar
                def _(a):
                    a.wait_ge(s.h["tn"], gu_base + m + 1)
                    a.wait_ge(s.h["ve"], s.get(uid + "_sg_used"))
                    s.inc(a.activation(sg[:], psG[:, m % 3, :], AF.Silu), "ac")
                sg_c = s.c["ac"]
                @block.vector
                def _(v):
                    v.wait_ge(s.h["ac"], sg_c)
                    s.inc(v.tensor_tensor(h2m[:, m * 512:(m + 1) * 512], sg[:], psU[:, m % 3, :], OP.mult), "ve")
                    s.c[uid + "_sg_used"] = s.c["ve"]
            h2_done = s.c["ve"]
            dev_base = s.c["ac"]
            @block.tensor
            def _(tn):
                tn.wait_ge(s.h["ve"], h2_done)
                for d in range(DC):
                    if d >= 2:
                        tn.wait_ge(s.h["ac"], dev_base + d - 1)
                    for k2 in range(MI):
                        i = tn.matmul(psD[:, d % 2, :], wd[:, k2 * 1024 + d * 128: k2 * 1024 + d * 128 + 128],
                                      h2m[:, k2 * 512:(k2 + 1) * 512],
                                      start=(k2 == 0), stop=(k2 == MI - 1))
                    s.inc(i, "tn")
            dn_base = s.c["tn"] - DC
            @block.scalar
            def _(a):
                a.wait_ge(s.h["qw"], s.get(uid + "_t_st"))
                for d in range(DC):
                    a.wait_ge(s.h["tn"], dn_base + d + 1)
                    s.inc(a.copy(tmp[:, d * 512:(d + 1) * 512], psD[:, d % 2, :]), "ac")
            ev_c = s.c["ac"]
            @block.gpsimd
            def _(gq):
                gq.wait_ge(s.h["ac"], ev_c)
                s.inc(gq.dma_start(
                    out=acc_hbm.rearrange("(c p) l -> p c l", p=128)[:, :, sb * 512:(sb + 1) * 512],
                    in_=tmp[:].rearrange("p (c l) -> p c l", c=DC)), "qw", 16)
                s.c[uid + "_t_st"] = s.c["qw"]
    phase_barrier(block, s)


def emit_xl(nc, block, s, cst, xcst, x_in, x_out, W, scratch):
    emit_xl_phaseA(nc, block, s, cst, x_in, W, scratch["zbuf"],
                   scratch["qb"], scratch["kb"], scratch["vb"], scratch["gi"], scratch["gf"])
    with ExitStack() as _es:
        la = _es.enter_context(_sb(nc, "XL_la", [16, T], F32))
        ci = _es.enter_context(_sb(nc, "XL_ci", [16, T], F32))
        emit_xl_phaseAb(nc, block, s, scratch["gi"], scratch["gf"], la, ci)
        emit_xl_phaseB(nc, block, s, cst, xcst, scratch["qb"], scratch["kb"], scratch["vb"],
                       la, ci, scratch["hb"], dbg=scratch.get("dbg"))
    emit_fm_matmul_pass(nc, block, s, "WO", scratch["hb"], W["wo"], scratch["acc"])
    emit_residual(nc, block, s, cst, scratch["acc"], scratch["zbuf"], scratch["z1buf"], 1.0)
    emit_rms_transpose_pass(nc, block, s, cst, "R2", scratch["z1buf"], W["ln2_w"], scratch["h2"])
    emit_mlp_pass(nc, block, s, "ML", scratch["h2"], W, scratch["acc"])
    emit_residual(nc, block, s, cst, scratch["acc"], x_in, x_out, 1.0, add_hbm=scratch["z1buf"])


def build_xl_launch(debug=False):
    nc = bass.Bass("TRN2", target_bir_lowering=False)
    cst = decl_consts(nc)
    xcst = decl_xl_consts(nc)
    x_in = nc.dram_tensor("x_in", [T, D], F32, kind="ExternalInput").ap()
    x_out = nc.dram_tensor("x_out", [T, D], F32, kind="ExternalOutput").ap()
    W = xl_weight_decls(nc)
    sk = "ExternalOutput" if debug else "Internal"
    scratch = {
        "zbuf": nc.dram_tensor("scr_z", [T, D], F32, kind=sk).ap(),
        "z1buf": nc.dram_tensor("scr_z1", [T, D], F32, kind=sk).ap(),
        "qb": nc.dram_tensor("scr_q", [D, T], BF16, kind=sk).ap(),
        "kb": nc.dram_tensor("scr_k", [D, T], BF16, kind="Internal").ap(),
        "vb": nc.dram_tensor("scr_v", [D, T], BF16, kind="Internal").ap(),
        "gi": nc.dram_tensor("scr_gi", [16, T], F32, kind=sk).ap(),
        "gf": nc.dram_tensor("scr_gf", [16, T], F32, kind=sk).ap(),
        "hb": nc.dram_tensor("scr_hb", [D, T], BF16, kind=sk).ap(),
        "h2": nc.dram_tensor("scr_h2", [D, T], BF16, kind="Internal").ap(),
        "acc": nc.dram_tensor("scr_acc", [D, T], F32, kind="Internal").ap(),
    }
    dbg = None
    if debug:
        dbg = {
            "dsb": nc.dram_tensor("dbg_dsb", [128, 2048], F32, kind="ExternalOutput").ap(),
            "psb": nc.dram_tensor("dbg_psb", [128, 2048], BF16, kind="ExternalOutput").ap(),
            "qsc": nc.dram_tensor("dbg_qsc", [64, 2048], BF16, kind="ExternalOutput").ap(),
            "recB": nc.dram_tensor("dbg_recB", [64, 2048], F32, kind="ExternalOutput").ap(),
            "vtm": nc.dram_tensor("dbg_vtm", [128, 1024], BF16, kind="ExternalOutput").ap(),
            "explaB": nc.dram_tensor("dbg_explaB", [128, 2048], BF16, kind="ExternalOutput").ap(),
            "cst0": nc.dram_tensor("dbg_cst0", [64, 1024], F32, kind="ExternalOutput").ap(),
            "cst17": nc.dram_tensor("dbg_cst17", [64, 1024], F32, kind="ExternalOutput").ap(),
            "kdec0": nc.dram_tensor("dbg_kdec0", [128, 1024], BF16, kind="ExternalOutput").ap(),
            "gsT0": nc.dram_tensor("dbg_gsT0", [128, 16], F32, kind="ExternalOutput").ap(),
        }
    with nc.Block() as block:
        s = Sems(nc, SEM_NAMES + ["pos"])
        scratch["dbg"] = dbg
        emit_xl(nc, block, s, cst, xcst, x_in, x_out, W, scratch)
        @block.scalar
        def _(a):
            a.wait_ge(s.h["qs"], s.c["qs"])
        s.close()
    return nc


def kernel_unfused(**inputs):
    consts = host_consts()
    consts.update(ssm_consts())
    consts.update(xl_consts())
    x = np.ascontiguousarray(inputs["L0_attn_x"])
    xs = [np.ascontiguousarray(x[c]) for c in range(NC_)]

    nc_moe0 = build_moe_launch("l0")
    nc_ssm = build_ssm_launch()
    nc_moe1 = build_moe_launch("l1")
    nc_xl = build_xl_launch()

    def run(nc, extra):
        in_maps = []
        for c in range(NC_):
            m = dict(consts)
            m["x_in"] = xs[c]
            m.update(extra)
            in_maps.append(m)
        res = run_bass_kernel_spmd(nc, in_maps, core_ids=list(range(NC_)))
        for c in range(NC_):
            xs[c] = np.ascontiguousarray(res.results[c]["x_out"])

    w_moe0 = moe_weight_inmap(inputs, "l0", 0)
    w_moe1 = moe_weight_inmap(inputs, "l1", 1)
    run(nc_moe0, w_moe0)
    run(nc_ssm, ssm_weight_inmap(inputs))
    run(nc_moe0, w_moe0)
    run(nc_moe1, w_moe1)
    run(nc_xl, xl_weight_inmap(inputs))
    run(nc_moe1, w_moe1)
    return np.stack(xs, axis=0)



def build_fused():
    nc = bass.Bass("TRN2", target_bir_lowering=False)
    cst = decl_consts(nc)
    scst = decl_ssm_consts(nc)
    xcst = decl_xl_consts(nc)
    x_in = nc.dram_tensor("x_in", [T, D], F32, kind="ExternalInput").ap()
    x_out = nc.dram_tensor("x_out", [T, D], F32, kind="ExternalOutput").ap()
    xA = nc.dram_tensor("scr_xA", [T, D], F32, kind="Internal").ap()
    xB = nc.dram_tensor("scr_xB", [T, D], F32, kind="Internal").ap()
    Wm0 = moe_weight_decls(nc, "l0", pfx="m0_")
    Wss = ssm_weight_decls(nc, pfx="ss_")
    Wm1 = moe_weight_decls(nc, "l1", pfx="m1_")
    Wxl = xl_weight_decls(nc, pfx="xl_")
    moe_scr = {
        "xn": nc.dram_tensor("scr_xn", [D, T], BF16, kind="Internal").ap(),
        "acc": nc.dram_tensor("scr_acc", [D, T], F32, kind="Internal").ap(),
        "wem": nc.dram_tensor("scr_wem", [8, T], F32, kind="Internal").ap(),
    }
    ssm_scr = {
        "uln": nc.dram_tensor("scr_uln", [T, D], F32, kind="Internal").ap(),
        "kk": nc.dram_tensor("scr_kk", [T, D], F32, kind="Internal").ap(),
        "g1": nc.dram_tensor("scr_g1", [2, 2, 128, 128, D], BF16, kind="Internal").ap(),
        "sp": nc.dram_tensor("scr_sp", [2, 2, 128, 128, D], BF16, kind="Internal").ap(),
    }
    xl_scr = {
        "zbuf": nc.dram_tensor("scr_z", [T, D], F32, kind="Internal").ap(),
        "z1buf": nc.dram_tensor("scr_z1", [T, D], F32, kind="Internal").ap(),
        "qb": nc.dram_tensor("scr_q", [D, T], BF16, kind="Internal").ap(),
        "kb": nc.dram_tensor("scr_k", [D, T], BF16, kind="Internal").ap(),
        "vb": nc.dram_tensor("scr_v", [D, T], BF16, kind="Internal").ap(),
        "gi": nc.dram_tensor("scr_gi", [16, T], F32, kind="Internal").ap(),
        "gf": nc.dram_tensor("scr_gf", [16, T], F32, kind="Internal").ap(),
        "hb": nc.dram_tensor("scr_hb", [D, T], BF16, kind="Internal").ap(),
        "h2": nc.dram_tensor("scr_h2", [D, T], BF16, kind="Internal").ap(),
        "acc": moe_scr["acc"],
        "dbg": None,
    }
    with nc.Block() as block:
        s = Sems(nc, SEM_NAMES + ["pos"])
        emit_moe(nc, block, s, cst, x_in, xA, Wm0, "l0", moe_scr)
        emit_ssm(nc, block, s, cst, scst, xA, xB, Wss, ssm_scr)
        emit_moe(nc, block, s, cst, xB, xA, Wm0, "l0", moe_scr)
        emit_moe(nc, block, s, cst, xA, xB, Wm1, "l1", moe_scr)
        emit_xl(nc, block, s, cst, xcst, xB, xA, Wxl, xl_scr)
        emit_moe(nc, block, s, cst, xA, x_out, Wm1, "l1", moe_scr)
        @block.scalar
        def _(a):
            a.wait_ge(s.h["qs"], s.c["qs"])
        @block.gpsimd
        def _(g):
            g.wait_ge(s.h["qw"], s.c["qw"])
        s.close()
    return nc


def kernel(**inputs):
    consts = host_consts()
    consts.update(ssm_consts())
    consts.update(xl_consts())
    x = np.ascontiguousarray(inputs["L0_attn_x"])

    nc = build_fused()
    base = dict(consts)
    for k, v in moe_weight_inmap(inputs, "l0", 0).items():
        base["m0_" + k] = v
    for k, v in ssm_weight_inmap(inputs).items():
        base["ss_" + k] = v
    for k, v in moe_weight_inmap(inputs, "l1", 1).items():
        base["m1_" + k] = v
    for k, v in xl_weight_inmap(inputs).items():
        base["xl_" + k] = v
    in_maps = []
    for c in range(NC_):
        m = dict(base)
        m["x_in"] = np.ascontiguousarray(x[c])
        in_maps.append(m)
    res = run_bass_kernel_spmd(nc, in_maps, core_ids=list(range(NC_)))
    return np.stack([res.results[c]["x_out"] for c in range(NC_)], axis=0)
